# version 12
# baseline (speedup 1.0000x reference)
import numpy as np
from contextlib import ExitStack
import concourse.bass as bass
import concourse.mybir as mybir
from concourse.bass_utils import run_bass_kernel_spmd
import ml_dtypes

F32 = mybir.dt.float32
BF16 = mybir.dt.bfloat16
FP8 = mybir.dt.float8e4
AF = mybir.ActivationFunctionType
ALU = mybir.AluOpType
AX = mybir.AxisListType.X

D = 1024
DI = 2048
NH = 8
HD = 256
NST = 64
CH = 128
PROJ = 5128
LN_EPS = 1e-5
RMS_EPS = float(np.finfo(np.float32).eps)


def build_program(tpc, debug=False):
    nch = tpc // CH
    ntg = tpc // 128
    nq = max(1, tpc // 512)
    HSZ = NST * DI
    nc = bass.Bass("TRN2", target_bir_lowering=False)

    d_hid = nc.dram_tensor("hid", [tpc + 3, D], F32, kind="ExternalInput").ap()
    d_wni = nc.dram_tensor("w_norm_in", [128, 8], F32, kind="ExternalInput").ap()
    d_win = nc.dram_tensor("w_in_proj", [D, PROJ], F32, kind="ExternalInput").ap()
    d_cw = nc.dram_tensor("conv_w", [128, 64], F32, kind="ExternalInput").ap()
    d_cb = nc.dram_tensor("conv_b", [128, 16], F32, kind="ExternalInput").ap()
    d_wdt = nc.dram_tensor("w_dt", [NH, DI], F32, kind="ExternalInput").ap()
    d_bdt = nc.dram_tensor("b_dt", [128, 16], F32, kind="ExternalInput").ap()
    d_alog = nc.dram_tensor("A_log", [NH, 1], F32, kind="ExternalInput").ap()
    d_Dh = nc.dram_tensor("D_head", [1, NH], F32, kind="ExternalInput").ap()
    d_wno = nc.dram_tensor("w_norm_out", [128, 16], F32, kind="ExternalInput").ap()
    d_wout = nc.dram_tensor("w_out_proj", [DI, D], F32, kind="ExternalInput").ap()
    d_wffn = nc.dram_tensor("ffn_w", [D, D], mybir.dt.uint8, kind="ExternalInput").ap()
    d_wsc = nc.dram_tensor("ffn_wscale", [1, 1], F32, kind="ExternalInput").ap()
    d_fb = nc.dram_tensor("ffn_bias", [1, D], F32, kind="ExternalInput").ap()
    d_smask = nc.dram_tensor("smask", [1, 4], F32, kind="ExternalInput").ap()
    d_I128f = nc.dram_tensor("I128f", [128, 128], F32, kind="ExternalInput").ap()
    d_I128b = nc.dram_tensor("I128b", [128, 128], BF16, kind="ExternalInput").ap()
    d_U128 = nc.dram_tensor("U128", [128, 128], F32, kind="ExternalInput").ap()
    d_oneh = nc.dram_tensor("oneh", [128, 64], F32, kind="ExternalInput").ap()
    d_ones = nc.dram_tensor("ones1", [1, 128], F32, kind="ExternalInput").ap()
    d_ccons = nc.dram_tensor("colconsts", [128, 6], F32, kind="ExternalInput").ap()
    d_icons = nc.dram_tensor("intconsts", [128, 4], mybir.dt.int32, kind="ExternalInput").ap()
    d_out = nc.dram_tensor("out", [tpc, D], F32, kind="ExternalOutput").ap()
    d_zscr = nc.dram_tensor("zscr", [tpc, DI], BF16, kind="Internal").ap()
    d_xscr = nc.dram_tensor("xscr", [tpc, DI], BF16, kind="Internal").ap()
    d_qscr = nc.dram_tensor("qscr", [D, tpc], BF16, kind="Internal").ap()
    d_xnscr = nc.dram_tensor("xnscr", [tpc, D], BF16, kind="Internal").ap()
    d_bscr = nc.dram_tensor("bscr", [64, 8 * tpc], BF16, kind="Internal").ap()
    d_cscr = nc.dram_tensor("cscr", [64, 8 * tpc], BF16, kind="Internal").ap()
    d_ccin = nc.dram_tensor("ccin", [HSZ + NH], F32, kind="Internal").ap()
    d_dbg = nc.dram_tensor("dbg_yg", [tpc, DI], BF16, kind="Internal").ap() if debug else None
    d_dbg2 = nc.dram_tensor("dbg_yn", [tpc, DI], BF16, kind="Internal").ap() if debug else None
    d_dbg3 = nc.dram_tensor("dbg_h", [64, DI], F32, kind="Internal").ap() if debug else None
    d_dbg4 = nc.dram_tensor("dbg_s", [tpc, D], F32, kind="Internal").ap() if debug else None
    d_dbg5 = nc.dram_tensor("dbg_g", [tpc, D], F32, kind="Internal").ap() if debug else None
    d_ccout = nc.dram_tensor("ccout", [4, HSZ + NH], F32, kind="Internal").ap()

    es = ExitStack()
    SB = lambda nm, shp, dt=F32: es.enter_context(nc.sbuf_tensor(nm, shp, dt))
    PSUMT = lambda nm, shp: es.enter_context(nc.psum_tensor(nm, shp, F32))

    with es:
        ENG = {'v': nc.vector, 's': nc.scalar, 't': nc.tensor, 'g': nc.gpsimd, 'y': nc.sync}
        QSEMS = {'L': ['L0', 'L1', 'L2', 'L3'], 'S': ['S0', 'S1']}
        sem = {}
        for lab in ('v', 's', 't', 'g') + tuple(QSEMS['L']) + tuple(QSEMS['S']):
            sem[lab] = es.enter_context(nc.semaphore("sem_" + lab))
        cnt = {lab: 0 for lab in sem}
        qrr = {'L': 0, 'S': 0}
        waited = {e: {} for e in ENG}
        last_w = {}
        readers = {}
        progs = {e: [] for e in ENG}

        def _wait(ceng, lab, val):
            w = waited[ceng]
            if w.get(lab, -1) >= val:
                return
            s_obj = sem[lab]
            progs[ceng].append(lambda eng, s_obj=s_obj, val=val: eng.wait_ge(s_obj, val))
            w[lab] = val

        def pre(eng, r, w):
            for k in r:
                lw = last_w.get(k)
                if lw and not (eng == 't' and lw[0] == 't'):
                    _wait(eng, lw[0], lw[1])
            for k in w:
                lw = last_w.get(k)
                if lw and not (eng == 't' and lw[0] == 't'):
                    _wait(eng, lw[0], lw[1])
                for lab, v in readers.get(k, {}).items():
                    if not (eng == 't' and lab == 't'):
                        _wait(eng, lab, v)

        def E(eng, op, *args, r=(), w=(), **kw):
            pre(eng, r, w)
            cnt[eng] += 1
            my = cnt[eng]
            s_obj = sem[eng]

            def emit(engobj, op=op, args=args, kw=kw, s_obj=s_obj):
                ins = getattr(engobj, op)(*args, **kw)
                ins.then_inc(s_obj, 1)
            progs[eng].append(emit)
            for k in w:
                last_w[k] = (eng, my)
                readers[k] = {}
            for k in r:
                readers.setdefault(k, {})[eng] = my

        QENG = {'L': 'y', 'S': 's'}

        def DMA(q, out_ap, in_ap, r=(), w=()):
            eng = QENG[q]
            lab = QSEMS[q][qrr[q]]
            qrr[q] = (qrr[q] + 1) % len(QSEMS[q])
            pre(eng, r, w)
            _wait(eng, lab, cnt[lab])
            cnt[lab] += 16
            my = cnt[lab]
            s_obj = sem[lab]

            def emit(engobj, out_ap=out_ap, in_ap=in_ap, s_obj=s_obj):
                engobj.dma_start(out_ap, in_ap).then_inc(s_obj, 16)
            progs[eng].append(emit)
            for k in w:
                last_w[k] = (lab, my)
                readers[k] = {}
            for k in r:
                readers.setdefault(k, {})[lab] = my

        I128f = SB("sI128f", [128, 128]); I128b = SB("sI128b", [128, 128], BF16)
        U128 = SB("sU128", [128, 128])
        ones1 = SB("sones1", [1, 128]); ccons = SB("sccons", [128, 6])
        icons = SB("sicons", [128, 4], mybir.dt.int32)
        nc.const_aps.register(F32, 8388608.0, ccons[:, 4:5])
        nc.const_aps.register(F32, -8388608.0, ccons[:, 5:6])
        wni = SB("wni_s", [128, 8]); wno = SB("wno_s", [128, 16])
        cw_sb = SB("cw_sb", [128, 64]); cb_sb = SB("cb_sb", [128, 16])
        bdt_sb = SB("bdt_sb", [128, 16])
        alog_sb = SB("alog_sb", [8, 1]); negA = SB("negA", [8, 1])
        Dh_row = SB("Dh_row", [1, 8]); Dbc = SB("Dbc", [128, 8])
        fbias_row = SB("fbias_row", [1, D]); biasbc = SB("biasbc", [128, D])
        wsc_11 = SB("wsc_11", [1, 1]); wscbc = SB("wscbc", [128, 1])
        smask_sb = SB("smask_sb", [1, 4])
        Bc_b = [SB(f"Bc_b{i}", [64, 8 * 128], BF16) for i in range(2)]
        Cc_b = [SB(f"Cc_b{i}", [64, 8 * 128], BF16) for i in range(2)]
        explcs = SB("explcs", [128, nch * 8]); expneg = SB("expneg", [128, nch * 8])
        expneg_bf = SB("expneg_bf", [128, nch * 8], BF16)
        tdrow = SB("tdrow", [64, nch * 8])
        h_sb = SB("h_sb", [64, DI]); h_bf = SB("h_bf", [64, DI], BF16)
        Drow = SB("Drow", [1, 8]); el_last = SB("el_last", [1, 8])
        Dg = SB("Dg", [1, 32]); Dp = SB("Dp", [1, 32])
        coef = SB("coef", [1, 8]); cj_row = SB("cj_row", [1, 8]); cjt = SB("cjt", [64, 8])
        hid_tg = [SB(f"hid_tg{i}", [128, D]) for i in range(2)]
        xn_tg = [SB(f"xn_tg{i}", [128, D]) for i in range(2)]
        xn_bf = [SB(f"xn_bf{i}", [128, D], BF16) for i in range(2)]
        stat = [SB(f"stat{i}", [128, 24]) for i in range(4)]
        mv = [SB(f"mv{i}", [128, 2]) for i in range(4)]
        rr_c = [SB(f"rr_c{i}", [128, 1]) for i in range(4)]
        rry_c = SB("rry_c", [128, 1])
        diag = [SB(f"dg_{i}", [128, 128], BF16) for i in range(2)]
        dtA_tm = [SB(f"dtA_tm{i}", [128, 8]) for i in range(2)]

        Wst = [SB(f"Wst{i}", [128, 512]) for i in range(3)]
        Wbf = [SB(f"Wbf{i}", [128, 512], BF16) for i in range(3)]
        x_c = [SB(f"x_c{i}", [128, DI], BF16) for i in range(2)]
        dtA_fm = SB("dtA_fm", [8, tpc])
        xs_col = [SB(f"xs_col{i}", [128, 1]) for i in range(2)]
        xs_all = SB("xs_all", [128, 16])
        xsw = SB("xsw", [128, 1])
        xsr_col = SB("xsr_col", [128, 1])

        groupA = []

        def SBA(nm, shp, dt=F32):
            g = nc.sbuf_tensor(nm, shp, dt)
            t = g.__enter__()
            groupA.append(g)
            return t

        wdt_sb = SBA("wdt_sb", [8, DI])
        oneh = SBA("soneh", [128, 64])
        hid_halo = SBA("hid_halo", [3, D]); xnh = SBA("xnh", [3, D])
        xnh_bf = SBA("xnh_bf", [3, D], BF16)
        stath = SBA("stath", [3, 24]); mvh = SBA("mvh", [3, 2]); rrh_c = SBA("rrh_c", [3, 1])
        diag3 = SBA("diag3", [3, 3], BF16)
        xr_halo = SBA("xr_halo", [128, 24], BF16)
        Wz_bf = SBA("Wz_bf", [128, 8 * 512], BF16)
        xpre = [SBA(f"xpre{i}", [128, 3 + tpc]) for i in range(2)]
        cacc = [SBA(f"cacc{i}", [128, tpc]) for i in range(2)]
        sx_bf = [SBA(f"sx_bf{i}", [128, tpc], BF16) for i in range(2)]
        xev = [SBA(f"xev{i}", [128, 512], BF16) for i in range(2)]
        spbuf = SBA("spbuf", [128, 512])
        szst = [SBA(f"szst{i}", [128, 512], BF16) for i in range(2)]
        dt_raw = SBA("dt_raw", [8, tpc])
        xr_fm = SBA("xr_fm", [128, 8 * tpc], BF16)

        PS = [PSUMT(f"PS{i}", [128, 512]) for i in range(8)]
        PK = [f'PS{i}' for i in range(8)]
        PSB = [p[0:128, 0:64].bitcast(BF16) for p in PS]

        DMA('L', I128f[:], d_I128f, w=('I128f',))
        DMA('L', I128b[:], d_I128b, w=('I128b',))
        DMA('L', U128[:], d_U128, w=('U128',))
        DMA('L', oneh[:], d_oneh, w=('oneh',))
        DMA('L', ones1[:], d_ones, w=('ones1',))
        DMA('L', ccons[:], d_ccons, w=('ccons',))
        DMA('L', icons[:], d_icons, w=('icons',))
        DMA('L', wni[:], d_wni, w=('wni',))
        DMA('L', wno[:], d_wno, w=('wno',))
        DMA('L', cw_sb[:], d_cw, w=('cw',))
        DMA('L', cb_sb[:], d_cb, w=('cb',))
        DMA('L', wdt_sb[:], d_wdt, w=('wdt',))
        DMA('L', bdt_sb[:], d_bdt, w=('bdt',))
        DMA('L', alog_sb[:], d_alog, w=('alog',))
        DMA('L', Dh_row[:], d_Dh, w=('Dh',))
        DMA('L', fbias_row[:], d_fb, w=('fbias',))
        DMA('L', wsc_11[:], d_wsc, w=('wsc',))
        DMA('L', smask_sb[:], d_smask, w=('smask',))
        E('s', 'activation', negA[:], alog_sb[:], AF.Exp, r=('alog',), w=('negA',))
        E('v', 'tensor_scalar', negA[:], negA[:], -1.0 / 256.0, None, ALU.mult,
          r=('negA',), w=('negA',))
        E('t', 'matmul', PS[0][0:128, 0:8], ones1[0:1, :], Dh_row[:],
          start=True, stop=True, r=('ones1', 'Dh'), w=(PK[0],))
        E('v', 'tensor_copy', Dbc[:], PS[0][0:128, 0:8], r=(PK[0],), w=('Dbc',))
        E('t', 'matmul', PS[0][0:128, 0:1], ones1[0:1, :], wsc_11[:],
          start=True, stop=True, r=('ones1', 'wsc'), w=(PK[0],))
        E('v', 'tensor_copy', wscbc[:], PS[0][0:128, 0:1], r=(PK[0],), w=('wscbc',))
        for half in range(2):
            E('t', 'matmul', PS[half][0:128, 0:512], ones1[0:1, :],
              fbias_row[0:1, half * 512:(half + 1) * 512],
              start=True, stop=True, r=('ones1', 'fbias'), w=(PK[half],))
            E('v', 'tensor_copy', biasbc[:, half * 512:(half + 1) * 512],
              PS[half][0:128, 0:512], r=(PK[half],), w=('biasbc',))

        eps_ln = ccons[:, 0:1]
        eps_rms = ccons[:, 1:2]
        one_col = ccons[:, 2:3]
        e127_col = ccons[:, 3:4]

        def ln_rms(hid_buf, hkey, n_p, xn_o, xnkey, st, stkey, mvb, mvkey, rr_col, rrkey):
            nblk = D // 512
            for i in range(nblk):
                E('v', 'bn_stats', st[0:n_p, 6 * i:6 * i + 6],
                  hid_buf[0:n_p, 512 * i:512 * (i + 1)], r=(hkey,), w=(stkey,))
            E('v', 'bn_aggr', mvb[0:n_p, 0:2], st[0:n_p, 0:6 * nblk], r=(stkey,), w=(mvkey,))
            E('v', 'tensor_copy', rr_col[0:n_p, 0:1], mvb[0:n_p, 1:2],
              r=(mvkey,), w=(rrkey,))
            E('s', 'activation', mvb[0:n_p, 1:2], mvb[0:n_p, 1:2], AF.Sqrt,
              bias=eps_ln[0:n_p, 0:1], r=(mvkey, 'ccons'), w=(mvkey,))
            E('v', 'reciprocal', mvb[0:n_p, 1:2], mvb[0:n_p, 1:2], r=(mvkey,), w=(mvkey,))
            E('v', 'tensor_scalar', xn_o[0:n_p, :], hid_buf[0:n_p, :],
              mvb[0:n_p, 0:1], mvb[0:n_p, 1:2], ALU.subtract, ALU.mult,
              r=(hkey, mvkey), w=(xnkey,))
            E('v', 'tensor_scalar', rr_col[0:n_p, 0:1], rr_col[0:n_p, 0:1],
              mvb[0:n_p, 1:2], mvb[0:n_p, 1:2], ALU.mult, ALU.mult,
              r=(mvkey, rrkey), w=(rrkey,))
            E('s', 'activation', rr_col[0:n_p, 0:1], rr_col[0:n_p, 0:1], AF.Sqrt,
              bias=eps_rms[0:n_p, 0:1], r=(rrkey, 'ccons'), w=(rrkey,))
            E('v', 'reciprocal', rr_col[0:n_p, 0:1], rr_col[0:n_p, 0:1], r=(rrkey,), w=(rrkey,))

        DMA('L', hid_halo[:], d_hid[0:3, :], w=('hidh',))
        ln_rms(hid_halo, 'hidh', 3, xnh, 'xnh', stath, 'sth', mvh, 'mvh', rrh_c, 'rrh')
        E('s', 'copy', xnh_bf[:], xnh[:], r=('xnh',), w=('xnhb',))
        E('v', 'tensor_scalar', diag3[:], I128f[0:3, 0:3], rrh_c[0:3, 0:1], None,
          ALU.mult, r=('I128f', 'rrh'), w=('diag3',))
        for kt in range(8):
            E('t', 'matmul', PS[0][0:128, 0:3], xnh_bf[0:3, kt * 128:(kt + 1) * 128],
              diag3[:], start=True, stop=True, r=('xnhb', 'diag3'), w=(PK[0],))
            E('v', 'tensor_copy', xr_halo[:, kt * 3:kt * 3 + 3], PS[0][0:128, 0:3],
              r=(PK[0],), w=('xrh',))
        for tg in range(ntg):
            p = tg % 4
            hb = hid_tg[p % 2]
            hkey = f'hid{p % 2}'
            DMA('L', hb[:], d_hid[3 + tg * 128: 3 + (tg + 1) * 128, :], w=(hkey,))
            ln_rms(hb, hkey, 128, xn_tg[p % 2], f'xn{p % 2}', stat[p], f'st{p}', mv[p],
                   f'mv{p}', rr_c[p], f'rr{p}')
            E('s', 'copy', xn_bf[p % 2][:], xn_tg[p % 2][:], r=(f'xn{p % 2}',),
              w=(f'xnb{p % 2}',))
            E('v', 'tensor_scalar', diag[p % 2][:], I128f[:], rr_c[p][:, 0:1], None, ALU.mult,
              r=('I128f', f'rr{p}'), w=(f'diag{p % 2}',))
            for kt in range(8):
                bk = kt // 4
                E('t', 'matmul', PS[bk][0:128, (kt % 4) * 128:(kt % 4) * 128 + 128],
                  xn_bf[p % 2][:, kt * 128:(kt + 1) * 128], diag[p % 2][:],
                  start=True, stop=True, r=(f'xnb{p % 2}', f'diag{p % 2}'), w=(PK[bk],))
            xr_r = xr_fm[:, :].rearrange("q (k t) -> q k t", k=8)
            for bk in range(2):
                E('v', 'tensor_copy',
                  xr_r[:, bk * 4:(bk + 1) * 4, tg * 128:(tg + 1) * 128],
                  PS[bk][0:128, 0:512].rearrange("q (k t) -> q k t", k=4),
                  r=(PK[bk],), w=('xr',))
            DMA('S', d_xnscr[tg * 128:(tg + 1) * 128, :], xn_bf[p % 2][:],
                r=(f'xnb{p % 2}',), w=('xnscr',))
            wzf = Wz_bf[:, :].bitcast(F32)
            slA = wzf[:, 0:1024]
            slB = wzf[:, 1024:2048]
            slAi = Wz_bf[:, :].bitcast(mybir.dt.int32)[:, 0:1024]
            slBi = Wz_bf[:, :].bitcast(mybir.dt.int32)[:, 1024:2048]
            E('v', 'reduce_max', xs_all[:, tg:tg + 1], xn_tg[p % 2][:], axis=AX,
              apply_absolute_value=True, r=(f'xn{p % 2}',), w=('xsall',))
            E('v', 'tensor_scalar', xs_all[:, tg:tg + 1], xs_all[:, tg:tg + 1],
              1e-12, 1.0 / 448.0, ALU.max, ALU.mult, r=('xsall',), w=('xsall',))
            E('v', 'reciprocal', xsr_col[:], xs_all[:, tg:tg + 1], r=('xsall',), w=('xsr',))
            E('s', 'activation', slA, xn_tg[p % 2][:], AF.Abs, scale=xsr_col[:],
              r=(f'xn{p % 2}', 'xsr'), w=('wz',))
            E('v', 'tensor_scalar', slBi, slAi, 23, None, ALU.arith_shift_right,
              r=('wz',), w=('wz',))
            E('v', 'tensor_scalar', slBi, slBi, 121, None, ALU.max, r=('wz',), w=('wz',))
            E('v', 'tensor_scalar', slBi, slBi, -1, 257, ALU.mult, ALU.add,
              r=('wz',), w=('wz',))
            E('v', 'tensor_scalar', slBi, slBi, 23, None, ALU.arith_shift_left,
              r=('wz',), w=('wz',))
            E('v', 'tensor_tensor', slA, slA, slB, ALU.mult, r=('wz',), w=('wz',))
            E('s', 'add', slA, slA, 8388608.0, r=('wz', 'ccons'), w=('wz',))
            E('s', 'add', slA, slA, -8388608.0, r=('wz', 'ccons'), w=('wz',))
            E('v', 'tensor_scalar', slBi, slBi, 23, None, ALU.arith_shift_right,
              r=('wz',), w=('wz',))
            E('v', 'tensor_scalar', slBi, slBi, -1, 254, ALU.mult, ALU.add,
              r=('wz',), w=('wz',))
            E('v', 'tensor_scalar', slBi, slBi, 23, None, ALU.arith_shift_left,
              r=('wz',), w=('wz',))
            E('v', 'tensor_tensor', slA, slA, slB, ALU.mult, r=('wz',), w=('wz',))
            E('s', 'activation', slB, xn_bf[p % 2][:], AF.Sign,
              r=(f'xnb{p % 2}', 'xnscr'), w=('wz',))
            E('v', 'tensor_tensor', xn_bf[p % 2][:], slA, slB, ALU.mult,
              r=('wz',), w=(f'xnb{p % 2}',))
            for kt in range(8):
                bk = kt // 4
                pbq = PS[bk][0:128, 0:512].bitcast(BF16)
                E('t', 'transpose', pbq[:, (kt % 4) * 128:(kt % 4) * 128 + 128],
                  xn_bf[p % 2][:, kt * 128:(kt + 1) * 128], I128b[:],
                  r=(f'xnb{p % 2}', 'I128b'), w=(PK[bk],))
            for bk in range(2):
                E('v', 'tensor_copy', xev[bk][:],
                  PS[bk][0:128, 0:512].bitcast(BF16)[:, 0:512],
                  r=(PK[bk],), w=(f'xev{bk}',))
                DMA('S',
                    d_qscr[bk * 512:(bk + 1) * 512, tg * 128:(tg + 1) * 128]
                    .rearrange("(a q) c -> q a c", q=128),
                    xev[bk][:, :].rearrange("q (a c) -> q a c", a=4),
                    r=(f'xev{bk}',), w=('qscr',))

        def load_w_chunk(bi, kt, c0, width):
            DMA('L', Wst[bi][:, 0:width],
                d_win[kt * 128:(kt + 1) * 128, c0:c0 + width], w=(f'wst{bi}',))
            E('s', 'activation', Wbf[bi][:, 0:width], Wst[bi][:, 0:width], AF.Copy,
              scale=wni[:, kt:kt + 1], r=(f'wst{bi}', 'wni'), w=(f'wbf{bi}',))

        for ct in range(16):
            for kt in range(8):
                bi = kt % 3
                load_w_chunk(bi, kt, 2048 + ct * 128, 128)
                E('t', 'matmul', PS[2][0:128, 0:3], Wbf[bi][:, 0:128],
                  xr_halo[:, kt * 3:kt * 3 + 3], start=(kt == 0), stop=(kt == 7),
                  r=(f'wbf{bi}', 'xrh'), w=(PK[2],))
                for q in range(nq):
                    E('t', 'matmul', PS[3 + q][0:128, 0:512], Wbf[bi][:, 0:128],
                      xr_fm[:, kt * tpc + q * 512: kt * tpc + (q + 1) * 512],
                      start=(kt == 0), stop=(kt == 7),
                      r=(f'wbf{bi}', 'xr'), w=(PK[3 + q],))
            p = ct % 2
            E('s', 'copy', xpre[p][:, 0:3], PS[2][0:128, 0:3], r=(PK[2],), w=(f'xpre{p}',))
            for q in range(nq):
                E('s', 'copy', xpre[p][:, 3 + q * 512: 3 + (q + 1) * 512],
                  PS[3 + q][0:128, 0:512], r=(PK[3 + q],), w=(f'xpre{p}',))
            E('v', 'tensor_scalar', cacc[p][:, 0:tpc], xpre[p][:, 0:tpc],
              cw_sb[:, ct * 4:ct * 4 + 1], None, ALU.mult, r=(f'xpre{p}', 'cw'), w=(f'cacc{p}',))
            for j in (1, 2, 3):
                E('v', 'scalar_tensor_tensor', cacc[p][:, 0:tpc], xpre[p][:, j:j + tpc],
                  cw_sb[:, ct * 4 + j:ct * 4 + j + 1], cacc[p][:, 0:tpc],
                  ALU.mult, ALU.add, r=(f'xpre{p}', 'cw', f'cacc{p}'), w=(f'cacc{p}',))
            E('s', 'activation', sx_bf[p][:, 0:tpc], cacc[p][:, 0:tpc], AF.Silu,
              bias=cb_sb[:, ct:ct + 1], r=(f'cacc{p}', 'cb'), w=(f'sx{p}',))
            for g in range(nch):
                bi = (g // 4) % 2
                pb = PS[bi][0:128, 0:512].bitcast(BF16)
                E('t', 'transpose', pb[:, (g % 4) * 128:(g % 4) * 128 + 128],
                  sx_bf[p][:, g * 128:(g + 1) * 128], I128b[:],
                  r=(f'sx{p}', 'I128b'), w=(PK[bi],))
                if g % 4 == 3:
                    E('s', 'copy', xev[bi][:], pb[:, 0:512], r=(PK[bi],),
                      w=(f'xev{bi}',))
                    DMA('S',
                        d_xscr[(g - 3) * 128:(g + 1) * 128, ct * 128:(ct + 1) * 128]
                        .rearrange("(a q) c -> q a c", q=128),
                        xev[bi][:, :].rearrange("q (a c) -> q a c", a=4),
                        r=(f'xev{bi}',), w=('xscr',))

        for dscr, dkey, base in ((d_bscr, 'bscr', 4096), (d_cscr, 'cscr', 4608)):
            for ct in range(4):
                for kt in range(8):
                    bi = kt % 3
                    load_w_chunk(bi, kt, base + ct * 128, 128)
                    for q in range(nq):
                        E('t', 'matmul', PS[3 + q][0:128, 0:512], Wbf[bi][:, 0:128],
                          xr_fm[:, kt * tpc + q * 512: kt * tpc + (q + 1) * 512],
                          start=(kt == 0), stop=(kt == 7),
                          r=(f'wbf{bi}', 'xr'), w=(PK[3 + q],))
                for q in range(nq):
                    sti = q % 2
                    E('s', 'copy', szst[sti][:], PS[3 + q][0:128, 0:512],
                      r=(PK[3 + q],), w=(f'szst{sti}',))
                    DMA('S', dscr[:, (2 * ct) * tpc + q * 512: (2 * ct) * tpc + (q + 1) * 512],
                        szst[sti][0:64, :], r=(f'szst{sti}',), w=(dkey,))
                    DMA('S', dscr[:, (2 * ct + 1) * tpc + q * 512: (2 * ct + 1) * tpc + (q + 1) * 512],
                        szst[sti][64:128, :], r=(f'szst{sti}',), w=(dkey,))

        for kt in range(8):
            bi = kt % 3
            load_w_chunk(bi, kt, 5120, 8)
            for q in range(nq):
                E('t', 'matmul', PS[3 + q][0:8, 0:512], Wbf[bi][:, 0:8],
                  xr_fm[:, kt * tpc + q * 512: kt * tpc + (q + 1) * 512],
                  start=(kt == 0), stop=(kt == 7),
                  r=(f'wbf{bi}', 'xr'), w=(PK[3 + q],))
        for q in range(nq):
            E('v', 'tensor_copy', dt_raw[:, q * 512:(q + 1) * 512],
              PS[3 + q][0:8, 0:512], r=(PK[3 + q],), w=('dt_raw',))
        for ct in range(16):
            for q in range(nq):
                E('t', 'matmul', PS[0][0:128, 0:512], wdt_sb[:, ct * 128:(ct + 1) * 128],
                  dt_raw[:, q * 512:(q + 1) * 512], start=True, stop=True,
                  r=('wdt', 'dt_raw'), w=(PK[0],))
                E('s', 'activation', spbuf[:, 0:512], PS[0][0:128, 0:512], AF.Exp,
                  bias=bdt_sb[:, ct:ct + 1], r=(PK[0], 'bdt'), w=('spbuf',))
                E('s', 'activation', spbuf[:, 0:512], spbuf[:, 0:512], AF.Ln,
                  bias=one_col[:, 0:1], r=('spbuf', 'ccons'), w=('spbuf',))
                E('t', 'matmul', PS[4 + q][0:8, 0:512],
                  oneh[:, (ct // 2) * 8:(ct // 2) * 8 + 8], spbuf[:, 0:512],
                  start=(ct == 0), stop=(ct == 15),
                  r=('oneh', 'spbuf'), w=(PK[4 + q],))
        for q in range(nq):
            E('v', 'tensor_scalar', dtA_fm[:, q * 512:(q + 1) * 512], PS[4 + q][0:8, 0:512],
              negA[0:8, 0:1], None, ALU.mult, r=(PK[4 + q], 'negA'), w=('dtA',))

        for zc in range(4):
            for kt in range(8):
                bi = kt % 3
                DMA('L', Wst[bi][:, 0:512],
                    d_win[kt * 128:(kt + 1) * 128, zc * 512:(zc + 1) * 512], w=(f'wst{bi}',))
                E('s', 'activation', Wz_bf[:, kt * 512:(kt + 1) * 512], Wst[bi][:, 0:512],
                  AF.Copy, scale=wni[:, kt:kt + 1], r=(f'wst{bi}', 'wni'), w=('wz',))
            for tg in range(ntg):
                for kt in range(8):
                    E('t', 'matmul', PS[tg % 2][0:128, 0:512],
                      xr_fm[:, kt * tpc + tg * 128:kt * tpc + (tg + 1) * 128],
                      Wz_bf[:, kt * 512:(kt + 1) * 512],
                      start=(kt == 0), stop=(kt == 7),
                      r=('xr', 'wz'), w=(PK[tg % 2],))
                bi = tg % 2
                E('s', 'activation', szst[bi][:], PS[tg % 2][0:128, 0:512], AF.Silu,
                  r=(PK[tg % 2],), w=(f'szst{bi}',))
                DMA('S', d_zscr[tg * 128:(tg + 1) * 128, zc * 512:(zc + 1) * 512],
                    szst[bi][:], r=(f'szst{bi}',), w=('zscr',))

        freed_keys = ('xr', 'xrh', 'wdt', 'oneh', 'hidh', 'xnh', 'xnhb', 'sth',
                      'mvh', 'rrh', 'diag3', 'wz', 'xpre0', 'xpre1', 'cacc0', 'cacc1',
                      'sx0', 'sx1', 'xev0', 'xev1', 'spbuf', 'szst0', 'szst1', 'dt_raw')
        pre('y', freed_keys, freed_keys)
        for g_ in reversed(groupA):
            g_.__exit__(None, None, None)
        groupB = []

        def SBB(nm, shp, dt=F32):
            g = nc.sbuf_tensor(nm, shp, dt)
            t = g.__enter__()
            groupB.append(g)
            return t

        Hg = SBB("Hg", [64, DI])
        Wout_bf = SBB("Wout_bf", [128, 16 * 1024], BF16)
        WffnT = SBB("WffnT", [128, 8 * 1024], BF16)
        wu8 = SBB("wu8", [128, 1024], mybir.dt.uint8)
        wf8b = SBB("wf8b", [128, 1024], BF16)
        db1 = SBB("db1", [128, 1024]); db2 = SBB("db2", [128, 1024])
        db3 = SBB("db3", [128, 1024]); db4 = SBB("db4", [128, 1024])
        dbi = SBB("dbi", [128, 1024], mybir.dt.int32)
        sz_c = [SBB(f"sz_c{i}", [128, DI], BF16) for i in range(2)]
        yg_sb = [SBB(f"yg_sb{i}", [128, DI], BF16) for i in range(2)]
        yn_fm = [SBB(f"yn_fm{i}", [128, DI], BF16) for i in range(2)]
        out_buf = [SBB(f"out_buf{i}", [128, D]) for i in range(2)]
        xnl = [SBB(f"xnl{i}", [128, D], BF16) for i in range(2)]
        B_tm4 = [SBB(f"B_tm4_{i}", [128, 64], BF16) for i in range(4)]
        Sbf4 = [SBB(f"Sbf4_{i}", [128, 128], BF16) for i in range(4)]
        wx4 = [SBB(f"wx4_{i}", [128, HD], BF16) for i in range(4)]
        yb4 = [SBB(f"yb4_{i}", [128, HD]) for i in range(4)]
        q_fm = [SBB(f"q_fm{i}", [128, D], BF16) for i in range(2)]

        for ct in range(16):
            for half in range(2):
                bi = half
                DMA('L', Wst[bi][:, 0:512],
                    d_wout[ct * 128:(ct + 1) * 128, half * 512:(half + 1) * 512],
                    w=(f'wst{bi}',))
                E('s', 'activation',
                  Wout_bf[:, ct * 1024 + half * 512: ct * 1024 + (half + 1) * 512],
                  Wst[bi][:, 0:512], AF.Copy, scale=wno[:, ct:ct + 1],
                  r=(f'wst{bi}', 'wno'), w=('wout',))
        for ot in range(8):
            DMA('L', wu8[:], d_wffn[ot * 128:(ot + 1) * 128, :], w=('wf8',))
            E('s', 'copy', db1[:], wu8[:], r=('wf8',), w=('db1',))
            E('v', 'tensor_scalar', db2[:], db1[:], 127.5, None, ALU.is_ge,
              r=('db1',), w=('db2',))
            E('v', 'scalar_tensor_tensor', db3[:], db2[:], -128.0, db1[:],
              ALU.mult, ALU.add, r=('db1', 'db2'), w=('db3',))
            E('v', 'tensor_scalar', db2[:], db2[:], -2.0, 1.0, ALU.mult, ALU.add,
              r=('db2',), w=('db2',))
            E('v', 'tensor_scalar', db4[:], db3[:], 0.125, -0.4375, ALU.mult, ALU.add,
              r=('db3',), w=('db4',))
            E('s', 'add', db4[:], db4[:], 8388608.0, r=('db4', 'ccons'), w=('db4',))
            E('s', 'add', db4[:], db4[:], -8388608.0, r=('db4', 'ccons'), w=('db4',))
            E('v', 'scalar_tensor_tensor', db4[:], db4[:], -8.0, db3[:],
              ALU.mult, ALU.add, r=('db3', 'db4'), w=('db4',))
            E('v', 'scalar_tensor_tensor', db3[:], db4[:], -1.0, db3[:],
              ALU.mult, ALU.add, r=('db3', 'db4'), w=('db3',))
            E('v', 'tensor_scalar', db3[:], db3[:], 0.125, None, ALU.mult,
              r=('db3',), w=('db3',))
            E('v', 'tensor_scalar', db1[:], db3[:], 8.0, 960.0,
              ALU.mult, ALU.add, r=('db3',), w=('db1',))
            E('v', 'tensor_tensor', db1[:], db1[:], db4[:], ALU.add,
              r=('db1', 'db4'), w=('db1',))
            E('s', 'copy', dbi[:], db1[:], r=('db1',), w=('dbi',))
            E('v', 'tensor_scalar', dbi[:], dbi[:], 20, None, ALU.arith_shift_left,
              r=('dbi',), w=('dbi',))
            E('v', 'tensor_scalar', db3[:], db3[:], 0.5, None, ALU.is_ge,
              r=('db3',), w=('db3',))
            E('v', 'tensor_scalar', db4[:], db4[:], 0.001953125, None, ALU.mult,
              r=('db4',), w=('db4',))
            E('v', 'tensor_tensor', db1[:], dbi[:, :].bitcast(F32),
              db4[:], ALU.subtract, r=('dbi', 'db4'), w=('db1',))
            E('v', 'tensor_tensor', db1[:], db1[:], db3[:], ALU.mult,
              r=('db1', 'db3'), w=('db1',))
            E('v', 'tensor_tensor', db1[:], db1[:], db4[:],
              ALU.add, r=('db1', 'db4'), w=('db1',))
            E('v', 'tensor_tensor', wf8b[:], db1[:], db2[:], ALU.mult,
              r=('db1', 'db2'), w=('wf8b',))
            for kt in range(8):
                E('t', 'transpose', PSB[kt % 2], wf8b[:, kt * 128:(kt + 1) * 128], I128b[:],
                  r=('wf8b', 'I128b'), w=(PK[kt % 2],))
                E('s', 'copy', WffnT[:, kt * 1024 + ot * 128: kt * 1024 + (ot + 1) * 128],
                  PSB[kt % 2], r=(PK[kt % 2],), w=('wffnT',))

        bscr_r = d_bscr.rearrange("p (h t) -> p h t", h=8)
        cscr_r = d_cscr.rearrange("p (h t) -> p h t", h=8)
        Bc_r = [b_[:, :].rearrange("p (h t) -> p h t", h=8) for b_ in Bc_b]
        Cc_r = [c_[:, :].rearrange("p (h t) -> p h t", h=8) for c_ in Cc_b]
        HKEYS = tuple(f'hsb{h}' for h in range(NH))
        E('v', 'memset', h_sb[:], 0.0, w=HKEYS)
        DMA('L', x_c[0][:], d_xscr[0:128, :], r=('xscr',), w=('xc0',))
        DMA('L', Bc_r[0], bscr_r[:, :, 0:128], r=('bscr',), w=('Bc0',))
        for c in range(nch):
            xcb = x_c[c % 2]
            xckey = f'xc{c % 2}'
            if c + 1 < nch:
                DMA('L', x_c[(c + 1) % 2][:], d_xscr[(c + 1) * 128:(c + 2) * 128, :],
                    r=('xscr',), w=(f'xc{(c + 1) % 2}',))
                DMA('L', Bc_r[(c + 1) % 2], bscr_r[:, :, (c + 1) * 128:(c + 2) * 128],
                    r=('bscr',), w=(f'Bc{(c + 1) % 2}',))
            E('t', 'transpose', PS[0][0:128, (c % 2) * 8:(c % 2) * 8 + 8],
              dtA_fm[:, c * 128:(c + 1) * 128],
              I128f[0:8, 0:8], r=('dtA', 'I128f'), w=(PK[0],))
            E('v', 'tensor_copy', dtA_tm[c % 2][:], PS[0][0:128, (c % 2) * 8:(c % 2) * 8 + 8],
              r=(PK[0],), w=(f'dtA_tm{c % 2}',))
            E('t', 'matmul', PS[1][0:128, (c % 2) * 8:(c % 2) * 8 + 8], U128[:], dtA_tm[c % 2][:],
              start=True, stop=True, r=('U128', f'dtA_tm{c % 2}'), w=(PK[1],))
            E('s', 'activation', explcs[:, c * 8:(c + 1) * 8],
              PS[1][0:128, (c % 2) * 8:(c % 2) * 8 + 8], AF.Exp,
              r=(PK[1],), w=('explcs',))
            E('v', 'tensor_scalar', expneg[:, c * 8:(c + 1) * 8],
              PS[1][0:128, (c % 2) * 8:(c % 2) * 8 + 8],
              -1.0, None, ALU.mult, r=(PK[1],), w=('expneg',))
            E('s', 'activation', expneg[:, c * 8:(c + 1) * 8], expneg[:, c * 8:(c + 1) * 8],
              AF.Exp, r=('expneg',), w=('expneg',))
            E('s', 'copy', expneg_bf[:, c * 8:(c + 1) * 8], expneg[:, c * 8:(c + 1) * 8],
              r=('expneg',), w=('expneg_bf',))
            E('t', 'matmul', PS[2][0:1, 0:8], e127_col, explcs[:, c * 8:(c + 1) * 8],
              start=True, stop=True, r=('ccons', 'explcs'), w=(PK[2],))
            E('v', 'tensor_copy', el_last[:], PS[2][0:1, 0:8], r=(PK[2],), w=('el_last',))
            E('t', 'matmul', PS[2][0:64, 0:8], ones1[0:1, 0:64], el_last[:],
              start=True, stop=True, r=('ones1', 'el_last'), w=(PK[2],))
            E('v', 'tensor_copy', tdrow[:, c * 8:(c + 1) * 8], PS[2][0:64, 0:8],
              r=(PK[2],), w=('tdrow',))
            PSB3 = PS[3][0:128, 0:512].bitcast(BF16)
            for h in range(NH):
                E('t', 'transpose', PSB3[:, h * 64:h * 64 + 64],
                  Bc_b[c % 2][:, h * 128:(h + 1) * 128], I128b[0:64, 0:64],
                  r=(f'Bc{c % 2}', 'I128b'), w=(PK[3],))
            for g in range(2):
                hs = [4 * g + i for i in range(4)]
                for hh, h in enumerate(hs):
                    E('s', 'copy', B_tm4[hh][:], PSB3[:, h * 64:h * 64 + 64],
                      r=(PK[3],), w=(f'Bt{hh}',))
                for hh, h in enumerate(hs):
                    E('v', 'tensor_scalar', wx4[hh][:], xcb[:, h * HD:(h + 1) * HD],
                      expneg[:, c * 8 + h:c * 8 + h + 1], None, ALU.mult,
                      r=(xckey, 'expneg'), w=(f'wx{hh}',))
                for hh, h in enumerate(hs):
                    ps = PS[4 + 2 * g + hh // 2][0:64, (hh % 2) * HD:(hh % 2) * HD + HD]
                    E('t', 'matmul', ps, B_tm4[hh][:, 0:64], wx4[hh][:],
                      start=True, stop=True, r=(f'Bt{hh}', f'wx{hh}'),
                      w=(PK[4 + 2 * g + hh // 2],))
                for hh, h in enumerate(hs):
                    ps = PS[4 + 2 * g + hh // 2][0:64, (hh % 2) * HD:(hh % 2) * HD + HD]
                    hhs = h_sb[:, h * HD:(h + 1) * HD]
                    E('v', 'tensor_tensor', hhs, hhs, ps, ALU.add,
                      r=(f'hsb{h}', PK[4 + 2 * g + hh // 2]), w=(f'hsb{h}',))
                    E('v', 'tensor_scalar', hhs, hhs, tdrow[:, c * 8 + h:c * 8 + h + 1],
                      None, ALU.mult, r=(f'hsb{h}', 'tdrow'), w=(f'hsb{h}',))
        E('v', 'memset', Drow[:], 1.0, w=('Drow',))
        for c in range(nch):
            E('v', 'tensor_tensor', Drow[:], Drow[:], tdrow[0:1, c * 8:(c + 1) * 8],
              ALU.mult, r=('Drow', 'tdrow'), w=('Drow',))

        DMA('L', d_ccin[0:HSZ].rearrange("(p f) -> p f", p=64), h_sb[:],
            r=HKEYS, w=('ccin',))
        DMA('L', d_ccin[HSZ:HSZ + 8].rearrange("(a b) -> a b", a=1), Drow[:],
            r=('Drow',), w=('ccin',))
        pre('g', ('ccin',), ('ccout',))
        cnt['g'] += 1
        ccmy = cnt['g']

        def emit_cc(engobj):
            ins = engobj.collective_compute(
                "AllGather", ALU.bypass,
                replica_groups=[[0, 1, 2, 3], [4, 5, 6, 7]],
                ins=[d_ccin], outs=[d_ccout])
            ins.then_inc(sem['g'], 1)
        progs['g'].append(emit_cc)
        last_w['ccout'] = ('g', ccmy)
        readers['ccout'] = {}
        for j in range(4):
            DMA('L', Dg[:, j * 8:(j + 1) * 8], d_ccout[j:j + 1, HSZ:HSZ + 8],
                r=('ccout',), w=('Dga',))
        for j in range(4):
            E('v', 'tensor_scalar', Dp[:, j * 8:(j + 1) * 8], Dg[:, j * 8:(j + 1) * 8],
              1.0, smask_sb[0:1, j:j + 1], ALU.subtract, ALU.mult,
              r=('Dga', 'smask'), w=('Dpa',))
            E('v', 'tensor_scalar', Dp[:, j * 8:(j + 1) * 8], Dp[:, j * 8:(j + 1) * 8],
              1.0, None, ALU.add, r=('Dpa',), w=('Dpa',))
        E('v', 'memset', h_sb[:], 0.0, w=HKEYS)
        E('v', 'memset', coef[:], 1.0, w=('coef',))
        for j in (3, 2, 1, 0):
            DMA('L', Hg[:], d_ccout[j:j + 1, 0:HSZ].rearrange("a (p f) -> (a p) f", p=64),
                r=('ccout',), w=('Hga',))
            E('v', 'tensor_scalar', cj_row[:], coef[:], smask_sb[0:1, j:j + 1], None,
              ALU.mult, r=('coef', 'smask'), w=('cj',))
            E('t', 'matmul', PS[0][0:64, 0:8], ones1[0:1, 0:64], cj_row[:],
              start=True, stop=True, r=('ones1', 'cj'), w=(PK[0],))
            E('v', 'tensor_copy', cjt[:], PS[0][0:64, 0:8], r=(PK[0],), w=('cjt',))
            for h in range(NH):
                E('v', 'scalar_tensor_tensor', h_sb[:, h * HD:(h + 1) * HD],
                  Hg[:, h * HD:(h + 1) * HD], cjt[:, h:h + 1],
                  h_sb[:, h * HD:(h + 1) * HD], ALU.mult, ALU.add,
                  r=('Hga', 'cjt', f'hsb{h}'), w=(f'hsb{h}',))
            E('v', 'tensor_tensor', coef[:], coef[:], Dp[:, j * 8:(j + 1) * 8], ALU.mult,
              r=('coef', 'Dpa'), w=('coef',))

        if debug:
            DMA('S', d_dbg3[0:64, :], h_sb[:], r=HKEYS)

        DMA('L', sz_c[0][:], d_zscr[0:128, :], r=('zscr',), w=('sz0',))
        DMA('L', x_c[0][:], d_xscr[0:128, :], r=('xscr',), w=('xc0',))
        DMA('L', Bc_r[0], bscr_r[:, :, 0:128], r=('bscr',), w=('Bc0',))
        DMA('L', Cc_r[0], cscr_r[:, :, 0:128], r=('cscr',), w=('Cc0',))

        def scan_emit(c):
            cp = c % 2
            szb = sz_c[cp]
            szkey = f'sz{cp}'
            xcb = x_c[cp]
            xckey = f'xc{cp}'
            bkey = f'Bc{cp}'
            ckey = f'Cc{cp}'
            if c + 1 < nch:
                np1 = (c + 1) % 2
                DMA('L', sz_c[np1][:], d_zscr[(c + 1) * 128:(c + 2) * 128, :],
                    r=('zscr',), w=(f'sz{np1}',))
                DMA('L', x_c[np1][:], d_xscr[(c + 1) * 128:(c + 2) * 128, :],
                    r=('xscr',), w=(f'xc{np1}',))
                DMA('L', Bc_r[np1], bscr_r[:, :, (c + 1) * 128:(c + 2) * 128],
                    r=('bscr',), w=(f'Bc{np1}',))
                DMA('L', Cc_r[np1], cscr_r[:, :, (c + 1) * 128:(c + 2) * 128],
                    r=('cscr',), w=(f'Cc{np1}',))
            E('s', 'copy', h_bf[:], h_sb[:], r=HKEYS, w=('h_bf',))
            for g in range(2):
                hs = [4 * g + i for i in range(4)]
                for hh, h in enumerate(hs):
                    bsl = Bc_b[cp][:, h * 128:(h + 1) * 128]
                    csl = Cc_b[cp][:, h * 128:(h + 1) * 128]
                    E('t', 'matmul', PS[0][0:128, hh * 128:(hh + 1) * 128], bsl, csl,
                      start=True, stop=True, r=(bkey, ckey), w=(PK[0],))
                for hh, h in enumerate(hs):
                    E('s', 'activation', wx4[hh][:],
                      xcb[:, h * HD:(h + 1) * HD], AF.Copy,
                      scale=expneg[:, c * 8 + h:c * 8 + h + 1],
                      r=(xckey, 'expneg'), w=(f'wx{hh}',))
                for hh, h in enumerate(hs):
                    E('v', 'tensor_tensor', Sbf4[hh][:],
                      PS[0][0:128, hh * 128:(hh + 1) * 128], U128[:], ALU.mult,
                      r=(PK[0], 'U128'), w=(f'Sb{hh}',))
                for hh, h in enumerate(hs):
                    csl = Cc_b[cp][:, h * 128:(h + 1) * 128]
                    yb = PS[1 + hh // 2][0:128, (hh % 2) * HD:(hh % 2) * HD + HD]
                    E('t', 'matmul', yb, Sbf4[hh][:], wx4[hh][:], start=True, stop=False,
                      r=(f'Sb{hh}', f'wx{hh}'), w=(PK[1 + hh // 2],))
                    E('t', 'matmul', yb, csl, h_bf[:, h * HD:(h + 1) * HD],
                      start=False, stop=True, r=(ckey, 'h_bf'), w=(PK[1 + hh // 2],))
                PSB0 = PS[0][0:128, 0:512].bitcast(BF16)
                for hh, h in enumerate(hs):
                    bsl = Bc_b[cp][:, h * 128:(h + 1) * 128]
                    E('t', 'transpose', PSB0[:, hh * 128:hh * 128 + 64], bsl,
                      I128b[0:64, 0:64], r=(bkey, 'I128b'), w=(PK[0],))
                for hh, h in enumerate(hs):
                    E('s', 'copy', B_tm4[hh][:], PSB0[:, hh * 128:hh * 128 + 64],
                      r=(PK[0],), w=(f'Bt{hh}',))
                for hh, h in enumerate(hs):
                    ps = PS[3 + hh // 2][0:64, (hh % 2) * HD:(hh % 2) * HD + HD]
                    E('t', 'matmul', ps, B_tm4[hh][:, 0:64], wx4[hh][:],
                      start=True, stop=True, r=(f'Bt{hh}', f'wx{hh}'), w=(PK[3 + hh // 2],))
                for hh, h in enumerate(hs):
                    yb = PS[1 + hh // 2][0:128, (hh % 2) * HD:(hh % 2) * HD + HD]
                    elcol = explcs[:, c * 8 + h:c * 8 + h + 1]
                    E('s', 'activation', yb4[hh][:], yb, AF.Copy, scale=elcol,
                      r=(PK[1 + hh // 2], 'explcs'), w=(f'yb{hh}',))
                    E('v', 'scalar_tensor_tensor', yb4[hh][:],
                      xcb[:, h * HD:(h + 1) * HD], Dbc[:, h:h + 1], yb4[hh][:],
                      ALU.mult, ALU.add, r=(xckey, 'Dbc', f'yb{hh}'), w=(f'yb{hh}',))
                    E('v', 'tensor_tensor', yg_sb[cp][:, h * HD:(h + 1) * HD], yb4[hh][:],
                      szb[:, h * HD:(h + 1) * HD], ALU.mult, r=(f'yb{hh}', szkey),
                      w=(f'yg{cp}_{h}',))
                for hh, h in enumerate(hs):
                    ps = PS[3 + hh // 2][0:64, (hh % 2) * HD:(hh % 2) * HD + HD]
                    hhs = h_sb[:, h * HD:(h + 1) * HD]
                    E('v', 'tensor_tensor', hhs, hhs, ps, ALU.add,
                      r=(f'hsb{h}', PK[3 + hh // 2]), w=(f'hsb{h}',))
                    E('v', 'tensor_scalar', hhs, hhs, tdrow[:, c * 8 + h:c * 8 + h + 1],
                      None, ALU.mult, r=(f'hsb{h}', 'tdrow'), w=(f'hsb{h}',))

        def epi_emit(c):
            cp = c % 2
            YGK = tuple(f'yg{cp}_{h}' for h in range(NH))
            if debug:
                DMA('S', d_dbg[c * 128:(c + 1) * 128, :], yg_sb[cp][:], r=YGK)
            for i in range(4):
                E('v', 'bn_stats', stat[cp][:, 6 * i:6 * i + 6],
                  yg_sb[cp][:, 512 * i:512 * (i + 1)], r=YGK, w=(f'st{cp}',))
            E('v', 'bn_aggr', mv[cp][:, 0:2], stat[cp][:, 0:24], r=(f'st{cp}',), w=(f'mv{cp}',))
            E('v', 'tensor_scalar', rry_c[:], mv[cp][:, 0:1], mv[cp][:, 0:1], mv[cp][:, 1:2],
              ALU.mult, ALU.add, r=(f'mv{cp}',), w=('rry',))
            E('s', 'activation', rry_c[:], rry_c[:], AF.Sqrt,
              bias=eps_rms[:, 0:1], r=('rry', 'ccons'), w=('rry',))
            E('v', 'reciprocal', rry_c[:], rry_c[:], r=('rry',), w=('rry',))
            E('v', 'tensor_scalar', diag[cp][:], I128f[:], rry_c[:, 0:1], None, ALU.mult,
              r=('I128f', 'rry'), w=(f'diag{cp}',))
            yn_r = yn_fm[cp][:, :].rearrange("q (a t) -> q a t", a=16)
            for ct in range(16):
                E('t', 'matmul', PS[5][0:128, (ct % 4) * 128:(ct % 4) * 128 + 128],
                  yg_sb[cp][:, ct * 128:(ct + 1) * 128], diag[cp][:], start=True, stop=True,
                  r=YGK + (f'diag{cp}',), w=(PK[5],))
                if ct % 4 == 3:
                    E('s', 'copy', yn_r[:, ct - 3:ct + 1, :],
                      PS[5][0:128, 0:512].rearrange("q (a t) -> q a t", a=4),
                      r=(PK[5],), w=(f'yn{cp}',))
            hb = hid_tg[cp]
            hkey = f'hid{cp}'
            DMA('L', hb[:], d_hid[3 + c * 128: 3 + (c + 1) * 128, :], w=(hkey,))
            DMA('L', xnl[cp][:], d_xnscr[c * 128:(c + 1) * 128, :],
                r=('xnscr',), w=(f'xnl{cp}',))
            DMA('L',
                q_fm[cp][:, :].rearrange("q (k c) -> q k c", k=8),
                d_qscr[:, c * 128:(c + 1) * 128].rearrange("(k q) c -> q k c", q=128),
                r=('qscr',), w=(f'qfm{cp}',))
            for half in range(2):
                for kt in range(8):
                    E('t', 'matmul', PS[6 + half][0:128, 0:512],
                      q_fm[cp][:, kt * 128:(kt + 1) * 128],
                      WffnT[:, kt * 1024 + half * 512: kt * 1024 + (half + 1) * 512],
                      start=(kt == 0), stop=(kt == 7),
                      r=(f'qfm{cp}', 'wffnT'), w=(PK[6 + half],))
            E('v', 'tensor_tensor', xsw[:], xs_all[:, c:c + 1], wscbc[:], ALU.mult,
              r=('xsall', 'wscbc'), w=('xsw',))
            for half in range(2):
                sl = slice(half * 512, (half + 1) * 512)
                E('v', 'scalar_tensor_tensor', db4[:, sl], PS[6 + half][0:128, 0:512],
                  xsw[:], biasbc[:, sl], ALU.mult, ALU.add,
                  r=(PK[6 + half], 'xsw', 'biasbc'), w=('db4',))
                E('s', 'activation', db2[:, sl], db4[:, sl], AF.Silu, r=('db4',), w=('db2',))
            for half in range(2):
                for ct in range(16):
                    E('t', 'matmul', PS[6 + half][0:128, 0:512],
                      yn_fm[cp][:, ct * 128:(ct + 1) * 128],
                      Wout_bf[:, ct * 1024 + half * 512: ct * 1024 + (half + 1) * 512],
                      start=(ct == 0), stop=(ct == 15),
                      r=(f'yn{cp}', 'wout'), w=(PK[6 + half],))
            ob = out_buf[cp]
            okey = f'ob{cp}'
            for half in range(2):
                sl = slice(half * 512, (half + 1) * 512)
                E('v', 'tensor_tensor', db3[:, sl], PS[6 + half][0:128, 0:512],
                  xnl[cp][:, sl], ALU.add, r=(PK[6 + half], f'xnl{cp}'), w=('db3',))
                E('v', 'tensor_tensor', db4[:, sl], db3[:, sl], db2[:, sl], ALU.mult,
                  r=('db3', 'db2'), w=('db4',))
                E('v', 'tensor_tensor', ob[:, sl], db4[:, sl], hb[:, sl], ALU.add,
                  r=('db4', hkey), w=(okey,))
            DMA('S', d_out[c * 128:(c + 1) * 128, :], ob[:], r=(okey,))
            if debug:
                DMA('S', d_dbg2[c * 128:(c + 1) * 128, :], yn_fm[cp][:], r=(f'yn{cp}',))
                DMA('S', d_dbg4[c * 128:(c + 1) * 128, :], db3[:], r=('db3',))
                DMA('S', d_dbg5[c * 128:(c + 1) * 128, :], db2[:], r=('db2',))

        for c in range(nch):
            scan_emit(c)
            if c >= 1:
                epi_emit(c - 1)
        epi_emit(nch - 1)

        for lab in QSEMS['S'] + QSEMS['L']:
            _wait('y', lab, cnt[lab])

        for g_ in reversed(groupB):
            g_.__exit__(None, None, None)

        with nc.Block() as blk:
            blk.sync(lambda e: [f(e) for f in progs['y']])
            blk.vector(lambda e: [f(e) for f in progs['v']])
            blk.scalar(lambda e: [f(e) for f in progs['s']])
            blk.tensor(lambda e: [f(e) for f in progs['t']])
            blk.gpsimd(lambda e: [f(e) for f in progs['g']])

    return nc


_NC_CACHE = {}


def _get_nc(tpc):
    if tpc not in _NC_CACHE:
        _NC_CACHE[tpc] = build_program(tpc)
    return _NC_CACHE[tpc]


def make_consts():
    I128f = np.eye(128, dtype=np.float32)
    I128b = np.eye(128, dtype=ml_dtypes.bfloat16)
    U128 = np.triu(np.ones((128, 128), dtype=np.float32))
    oneh = np.zeros((128, 64), dtype=np.float32)
    for j in range(8):
        oneh[:, j * 8 + j] = 1.0
    ones1 = np.ones((1, 128), dtype=np.float32)
    ccons = np.zeros((128, 6), dtype=np.float32)
    ccons[:, 0] = LN_EPS
    ccons[:, 1] = RMS_EPS
    ccons[:, 2] = 1.0
    ccons[127, 3] = 1.0
    ccons[:, 4] = 8388608.0
    ccons[:, 5] = -8388608.0
    icons_np = np.zeros((128, 4), dtype=np.int32)
    icons_np[:, 0] = 8388608
    icons_np[:, 1] = 1006632960
    icons_np[:, 2] = 1048576
    icons_np[:, 3] = -25165824
    return I128f, I128b, U128, oneh, ones1, ccons, icons_np


def make_in_maps(tpc, hidden, wni, win, cw, cb, wdt, bdt, alog, dh, wno, wout,
                 wffn, wsc, fbias):
    I128f, I128b, U128, oneh, ones1, ccons, icons_np = make_consts()
    in_maps = []
    for core in range(8):
        b, s = core // 4, core % 4
        lo = s * tpc - 3
        if lo < 0:
            hid_k = np.concatenate(
                [np.zeros((3, D), np.float32), hidden[b, 0:(s + 1) * tpc]], axis=0)
        else:
            hid_k = hidden[b, lo:(s + 1) * tpc]
        smask = np.array([[1.0 if j < s else 0.0 for j in range(4)]], np.float32)
        in_maps.append({
            "hid": np.ascontiguousarray(hid_k, np.float32),
            "w_norm_in": np.ascontiguousarray(np.asarray(wni, np.float32).reshape(8, 128).T),
            "w_in_proj": np.ascontiguousarray(win, np.float32),
            "conv_w": np.ascontiguousarray(
                np.asarray(cw, np.float32).reshape(16, 128, 4).transpose(1, 0, 2).reshape(128, 64)),
            "conv_b": np.ascontiguousarray(np.asarray(cb, np.float32).reshape(16, 128).T),
            "w_dt": np.ascontiguousarray(wdt, np.float32),
            "b_dt": np.ascontiguousarray(np.asarray(bdt, np.float32).reshape(16, 128).T),
            "A_log": np.ascontiguousarray(np.asarray(alog, np.float32).reshape(NH, 1)),
            "D_head": np.ascontiguousarray(np.asarray(dh, np.float32).reshape(1, NH)),
            "w_norm_out": np.ascontiguousarray(np.asarray(wno, np.float32).reshape(16, 128).T),
            "w_out_proj": np.ascontiguousarray(wout, np.float32),
            "ffn_w": np.ascontiguousarray(np.asarray(wffn).view(np.uint8)),
            "ffn_wscale": np.ascontiguousarray(np.asarray(wsc, np.float32).reshape(1, 1)),
            "ffn_bias": np.ascontiguousarray(np.asarray(fbias, np.float32).reshape(1, D)),
            "smask": smask,
            "I128f": I128f, "I128b": I128b, "U128": U128, "oneh": oneh,
            "ones1": ones1, "colconsts": ccons, "intconsts": icons_np,
        })
    return in_maps


def kernel(L0_attn_hidden, L0_attn_w_norm_in, L0_attn_w_in_proj, L0_attn_conv_w,
           L0_attn_conv_b, L0_attn_w_dt, L0_attn_b_dt, L0_attn_A_log, L0_attn_D,
           L0_attn_w_norm_out, L0_attn_w_out_proj, L0_ffn_weight, L0_ffn_weight_scale,
           L0_ffn_bias):
    hidden = np.asarray(L0_attn_hidden, np.float32)
    Bz, S, _ = hidden.shape
    tpc = S // 4
    nc = _get_nc(tpc)
    in_maps = make_in_maps(
        tpc, hidden, L0_attn_w_norm_in, L0_attn_w_in_proj, L0_attn_conv_w,
        L0_attn_conv_b, L0_attn_w_dt, L0_attn_b_dt, L0_attn_A_log, L0_attn_D,
        L0_attn_w_norm_out, L0_attn_w_out_proj, L0_ffn_weight, L0_ffn_weight_scale,
        L0_ffn_bias)
    res = run_bass_kernel_spmd(nc, in_maps, core_ids=list(range(8)))
    out = np.empty((Bz, S, D), np.float32)
    for core in range(8):
        b, s = core // 4, core % 4
        out[b, s * tpc:(s + 1) * tpc] = res.results[core]["out"]
    return out


# version 13
# speedup vs baseline: 1.0053x; 1.0053x over previous
import numpy as np
from contextlib import ExitStack
import concourse.bass as bass
import concourse.mybir as mybir
from concourse.bass_utils import run_bass_kernel_spmd
import ml_dtypes

F32 = mybir.dt.float32
BF16 = mybir.dt.bfloat16
FP8 = mybir.dt.float8e4
AF = mybir.ActivationFunctionType
ALU = mybir.AluOpType
AX = mybir.AxisListType.X

D = 1024
DI = 2048
NH = 8
HD = 256
NST = 64
CH = 128
PROJ = 5128
LN_EPS = 1e-5
RMS_EPS = float(np.finfo(np.float32).eps)


def build_program(tpc, debug=False):
    nch = tpc // CH
    ntg = tpc // 128
    nq = max(1, tpc // 512)
    HSZ = NST * DI
    nc = bass.Bass("TRN2", target_bir_lowering=False)

    d_hid = nc.dram_tensor("hid", [tpc + 3, D], F32, kind="ExternalInput").ap()
    d_wni = nc.dram_tensor("w_norm_in", [128, 8], F32, kind="ExternalInput").ap()
    d_win = nc.dram_tensor("w_in_proj", [D, PROJ], F32, kind="ExternalInput").ap()
    d_cw = nc.dram_tensor("conv_w", [128, 64], F32, kind="ExternalInput").ap()
    d_cb = nc.dram_tensor("conv_b", [128, 16], F32, kind="ExternalInput").ap()
    d_wdt = nc.dram_tensor("w_dt", [NH, DI], F32, kind="ExternalInput").ap()
    d_bdt = nc.dram_tensor("b_dt", [128, 16], F32, kind="ExternalInput").ap()
    d_alog = nc.dram_tensor("A_log", [NH, 1], F32, kind="ExternalInput").ap()
    d_Dh = nc.dram_tensor("D_head", [1, NH], F32, kind="ExternalInput").ap()
    d_wno = nc.dram_tensor("w_norm_out", [128, 16], F32, kind="ExternalInput").ap()
    d_wout = nc.dram_tensor("w_out_proj", [DI, D], F32, kind="ExternalInput").ap()
    d_wffn = nc.dram_tensor("ffn_w", [D, D], mybir.dt.uint8, kind="ExternalInput").ap()
    d_wsc = nc.dram_tensor("ffn_wscale", [1, 1], F32, kind="ExternalInput").ap()
    d_fb = nc.dram_tensor("ffn_bias", [1, D], F32, kind="ExternalInput").ap()
    d_smask = nc.dram_tensor("smask", [1, 4], F32, kind="ExternalInput").ap()
    d_I128f = nc.dram_tensor("I128f", [128, 128], F32, kind="ExternalInput").ap()
    d_I128b = nc.dram_tensor("I128b", [128, 128], BF16, kind="ExternalInput").ap()
    d_U128 = nc.dram_tensor("U128", [128, 128], F32, kind="ExternalInput").ap()
    d_oneh = nc.dram_tensor("oneh", [128, 64], F32, kind="ExternalInput").ap()
    d_ones = nc.dram_tensor("ones1", [1, 128], F32, kind="ExternalInput").ap()
    d_ccons = nc.dram_tensor("colconsts", [128, 6], F32, kind="ExternalInput").ap()
    d_icons = nc.dram_tensor("intconsts", [128, 4], mybir.dt.int32, kind="ExternalInput").ap()
    d_out = nc.dram_tensor("out", [tpc, D], F32, kind="ExternalOutput").ap()
    d_zscr = nc.dram_tensor("zscr", [tpc, DI], BF16, kind="Internal").ap()
    d_xscr = nc.dram_tensor("xscr", [tpc, DI], BF16, kind="Internal").ap()
    d_qscr = nc.dram_tensor("qscr", [D, tpc], BF16, kind="Internal").ap()
    d_xnscr = nc.dram_tensor("xnscr", [tpc, D], BF16, kind="Internal").ap()
    d_bscr = nc.dram_tensor("bscr", [64, 8 * tpc], BF16, kind="Internal").ap()
    d_cscr = nc.dram_tensor("cscr", [64, 8 * tpc], BF16, kind="Internal").ap()
    d_ccin = nc.dram_tensor("ccin", [HSZ + NH], F32, kind="Internal").ap()
    d_dbg = nc.dram_tensor("dbg_yg", [tpc, DI], BF16, kind="Internal").ap() if debug else None
    d_dbg2 = nc.dram_tensor("dbg_yn", [tpc, DI], BF16, kind="Internal").ap() if debug else None
    d_dbg3 = nc.dram_tensor("dbg_h", [64, DI], F32, kind="Internal").ap() if debug else None
    d_dbg4 = nc.dram_tensor("dbg_s", [tpc, D], F32, kind="Internal").ap() if debug else None
    d_dbg5 = nc.dram_tensor("dbg_g", [tpc, D], F32, kind="Internal").ap() if debug else None
    d_ccout = nc.dram_tensor("ccout", [4, HSZ + NH], F32, kind="Internal").ap()

    es = ExitStack()
    SB = lambda nm, shp, dt=F32: es.enter_context(nc.sbuf_tensor(nm, shp, dt))
    PSUMT = lambda nm, shp: es.enter_context(nc.psum_tensor(nm, shp, F32))

    with es:
        ENG = {'v': nc.vector, 's': nc.scalar, 't': nc.tensor, 'g': nc.gpsimd, 'y': nc.sync}
        QSEMS = {'L': ['L0', 'L1', 'L2', 'L3'], 'S': ['S0', 'S1']}
        sem = {}
        for lab in ('v', 's', 't', 'g') + tuple(QSEMS['L']) + tuple(QSEMS['S']):
            sem[lab] = es.enter_context(nc.semaphore("sem_" + lab))
        cnt = {lab: 0 for lab in sem}
        qrr = {'L': 0, 'S': 0}
        waited = {e: {} for e in ENG}
        last_w = {}
        readers = {}
        progs = {e: [] for e in ENG}

        def _wait(ceng, lab, val):
            w = waited[ceng]
            if w.get(lab, -1) >= val:
                return
            s_obj = sem[lab]
            progs[ceng].append(lambda eng, s_obj=s_obj, val=val: eng.wait_ge(s_obj, val))
            w[lab] = val

        def pre(eng, r, w):
            for k in r:
                lw = last_w.get(k)
                if lw and not (eng == 't' and lw[0] == 't'):
                    _wait(eng, lw[0], lw[1])
            for k in w:
                lw = last_w.get(k)
                if lw and not (eng == 't' and lw[0] == 't'):
                    _wait(eng, lw[0], lw[1])
                for lab, v in readers.get(k, {}).items():
                    if not (eng == 't' and lab == 't'):
                        _wait(eng, lab, v)

        def E(eng, op, *args, r=(), w=(), **kw):
            pre(eng, r, w)
            cnt[eng] += 1
            my = cnt[eng]
            s_obj = sem[eng]

            def emit(engobj, op=op, args=args, kw=kw, s_obj=s_obj):
                ins = getattr(engobj, op)(*args, **kw)
                ins.then_inc(s_obj, 1)
            progs[eng].append(emit)
            for k in w:
                last_w[k] = (eng, my)
                readers[k] = {}
            for k in r:
                readers.setdefault(k, {})[eng] = my

        QENG = {'L': 'y', 'S': 's'}

        def DMA(q, out_ap, in_ap, r=(), w=()):
            eng = QENG[q]
            lab = QSEMS[q][qrr[q]]
            qrr[q] = (qrr[q] + 1) % len(QSEMS[q])
            pre(eng, r, w)
            _wait(eng, lab, cnt[lab])
            cnt[lab] += 16
            my = cnt[lab]
            s_obj = sem[lab]

            def emit(engobj, out_ap=out_ap, in_ap=in_ap, s_obj=s_obj):
                engobj.dma_start(out_ap, in_ap).then_inc(s_obj, 16)
            progs[eng].append(emit)
            for k in w:
                last_w[k] = (lab, my)
                readers[k] = {}
            for k in r:
                readers.setdefault(k, {})[lab] = my

        I128f = SB("sI128f", [128, 128]); I128b = SB("sI128b", [128, 128], BF16)
        U128 = SB("sU128", [128, 128])
        ones1 = SB("sones1", [1, 128]); ccons = SB("sccons", [128, 6])
        icons = SB("sicons", [128, 4], mybir.dt.int32)
        nc.const_aps.register(F32, 8388608.0, ccons[:, 4:5])
        nc.const_aps.register(F32, -8388608.0, ccons[:, 5:6])
        wni = SB("wni_s", [128, 8]); wno = SB("wno_s", [128, 16])
        cw_sb = SB("cw_sb", [128, 64]); cb_sb = SB("cb_sb", [128, 16])
        bdt_sb = SB("bdt_sb", [128, 16])
        alog_sb = SB("alog_sb", [8, 1]); negA = SB("negA", [8, 1])
        Dh_row = SB("Dh_row", [1, 8]); Dbc = SB("Dbc", [128, 8])
        fbias_row = SB("fbias_row", [1, D]); biasbc = SB("biasbc", [128, D])
        wsc_11 = SB("wsc_11", [1, 1]); wscbc = SB("wscbc", [128, 1])
        smask_sb = SB("smask_sb", [1, 4])
        Bc_b = [SB(f"Bc_b{i}", [64, 8 * 128], BF16) for i in range(2)]
        Cc_b = [SB(f"Cc_b{i}", [64, 8 * 128], BF16) for i in range(2)]
        explcs = SB("explcs", [128, nch * 8]); expneg = SB("expneg", [128, nch * 8])
        expneg_bf = SB("expneg_bf", [128, nch * 8], BF16)
        tdrow = SB("tdrow", [64, nch * 8])
        h_sb = SB("h_sb", [64, DI]); h_bf = SB("h_bf", [64, DI], BF16)
        Drow = SB("Drow", [1, 8]); el_last = SB("el_last", [1, 8])
        Dg = SB("Dg", [1, 32]); Dp = SB("Dp", [1, 32])
        coef = SB("coef", [1, 8]); cj_row = SB("cj_row", [1, 8]); cjt = SB("cjt", [64, 8])
        hid_tg = [SB(f"hid_tg{i}", [128, D]) for i in range(2)]
        xn_tg = [SB(f"xn_tg{i}", [128, D]) for i in range(2)]
        xn_bf = [SB(f"xn_bf{i}", [128, D], BF16) for i in range(2)]
        stat = [SB(f"stat{i}", [128, 24]) for i in range(4)]
        mv = [SB(f"mv{i}", [128, 2]) for i in range(4)]
        rr_c = [SB(f"rr_c{i}", [128, 1]) for i in range(4)]
        rry_c = SB("rry_c", [128, 1])
        diag = [SB(f"dg_{i}", [128, 128], BF16) for i in range(2)]
        dtA_tm = [SB(f"dtA_tm{i}", [128, 8]) for i in range(2)]

        Wst = [SB(f"Wst{i}", [128, 512]) for i in range(3)]
        Wbf = [SB(f"Wbf{i}", [128, 512], BF16) for i in range(3)]
        x_c = [SB(f"x_c{i}", [128, DI], BF16) for i in range(2)]
        dtA_fm = SB("dtA_fm", [8, tpc])
        xs_col = [SB(f"xs_col{i}", [128, 1]) for i in range(2)]
        xs_all = SB("xs_all", [128, 16])
        xsw = SB("xsw", [128, 1])
        xsr_col = SB("xsr_col", [128, 1])

        groupA = []

        def SBA(nm, shp, dt=F32):
            g = nc.sbuf_tensor(nm, shp, dt)
            t = g.__enter__()
            groupA.append(g)
            return t

        wdt_sb = SBA("wdt_sb", [8, DI])
        oneh = SBA("soneh", [128, 64])
        hid_halo = SBA("hid_halo", [3, D]); xnh = SBA("xnh", [3, D])
        xnh_bf = SBA("xnh_bf", [3, D], BF16)
        stath = SBA("stath", [3, 24]); mvh = SBA("mvh", [3, 2]); rrh_c = SBA("rrh_c", [3, 1])
        diag3 = SBA("diag3", [3, 3], BF16)
        xr_halo = SBA("xr_halo", [128, 24], BF16)
        Wz_bf = SBA("Wz_bf", [128, 8 * 512], BF16)
        xpre = [SBA(f"xpre{i}", [128, 3 + tpc]) for i in range(2)]
        cacc = [SBA(f"cacc{i}", [128, tpc]) for i in range(2)]
        sx_bf = [SBA(f"sx_bf{i}", [128, tpc], BF16) for i in range(2)]
        xev = [SBA(f"xev{i}", [128, 512], BF16) for i in range(2)]
        spbuf = SBA("spbuf", [128, 512])
        szst = [SBA(f"szst{i}", [128, 512], BF16) for i in range(2)]
        dt_raw = SBA("dt_raw", [8, tpc])
        xr_fm = SBA("xr_fm", [128, 8 * tpc], BF16)

        PS = [PSUMT(f"PS{i}", [128, 512]) for i in range(8)]
        PK = [f'PS{i}' for i in range(8)]
        PSB = [p[0:128, 0:64].bitcast(BF16) for p in PS]

        DMA('L', I128f[:], d_I128f, w=('I128f',))
        DMA('L', I128b[:], d_I128b, w=('I128b',))
        DMA('L', U128[:], d_U128, w=('U128',))
        DMA('L', oneh[:], d_oneh, w=('oneh',))
        DMA('L', ones1[:], d_ones, w=('ones1',))
        DMA('L', ccons[:], d_ccons, w=('ccons',))
        DMA('L', icons[:], d_icons, w=('icons',))
        DMA('L', wni[:], d_wni, w=('wni',))
        DMA('L', wno[:], d_wno, w=('wno',))
        DMA('L', cw_sb[:], d_cw, w=('cw',))
        DMA('L', cb_sb[:], d_cb, w=('cb',))
        DMA('L', wdt_sb[:], d_wdt, w=('wdt',))
        DMA('L', bdt_sb[:], d_bdt, w=('bdt',))
        DMA('L', alog_sb[:], d_alog, w=('alog',))
        DMA('L', Dh_row[:], d_Dh, w=('Dh',))
        DMA('L', fbias_row[:], d_fb, w=('fbias',))
        DMA('L', wsc_11[:], d_wsc, w=('wsc',))
        DMA('L', smask_sb[:], d_smask, w=('smask',))
        E('s', 'activation', negA[:], alog_sb[:], AF.Exp, r=('alog',), w=('negA',))
        E('v', 'tensor_scalar', negA[:], negA[:], -1.0 / 256.0, None, ALU.mult,
          r=('negA',), w=('negA',))
        E('t', 'matmul', PS[0][0:128, 0:8], ones1[0:1, :], Dh_row[:],
          start=True, stop=True, r=('ones1', 'Dh'), w=(PK[0],))
        E('v', 'tensor_copy', Dbc[:], PS[0][0:128, 0:8], r=(PK[0],), w=('Dbc',))
        E('t', 'matmul', PS[0][0:128, 0:1], ones1[0:1, :], wsc_11[:],
          start=True, stop=True, r=('ones1', 'wsc'), w=(PK[0],))
        E('v', 'tensor_copy', wscbc[:], PS[0][0:128, 0:1], r=(PK[0],), w=('wscbc',))
        for half in range(2):
            E('t', 'matmul', PS[half][0:128, 0:512], ones1[0:1, :],
              fbias_row[0:1, half * 512:(half + 1) * 512],
              start=True, stop=True, r=('ones1', 'fbias'), w=(PK[half],))
            E('v', 'tensor_copy', biasbc[:, half * 512:(half + 1) * 512],
              PS[half][0:128, 0:512], r=(PK[half],), w=('biasbc',))

        eps_ln = ccons[:, 0:1]
        eps_rms = ccons[:, 1:2]
        one_col = ccons[:, 2:3]
        e127_col = ccons[:, 3:4]

        def ln_rms(hid_buf, hkey, n_p, xn_o, xnkey, st, stkey, mvb, mvkey, rr_col, rrkey):
            nblk = D // 512
            for i in range(nblk):
                E('v', 'bn_stats', st[0:n_p, 6 * i:6 * i + 6],
                  hid_buf[0:n_p, 512 * i:512 * (i + 1)], r=(hkey,), w=(stkey,))
            E('v', 'bn_aggr', mvb[0:n_p, 0:2], st[0:n_p, 0:6 * nblk], r=(stkey,), w=(mvkey,))
            E('v', 'tensor_copy', rr_col[0:n_p, 0:1], mvb[0:n_p, 1:2],
              r=(mvkey,), w=(rrkey,))
            E('s', 'activation', mvb[0:n_p, 1:2], mvb[0:n_p, 1:2], AF.Sqrt,
              bias=eps_ln[0:n_p, 0:1], r=(mvkey, 'ccons'), w=(mvkey,))
            E('v', 'reciprocal', mvb[0:n_p, 1:2], mvb[0:n_p, 1:2], r=(mvkey,), w=(mvkey,))
            E('v', 'tensor_scalar', xn_o[0:n_p, :], hid_buf[0:n_p, :],
              mvb[0:n_p, 0:1], mvb[0:n_p, 1:2], ALU.subtract, ALU.mult,
              r=(hkey, mvkey), w=(xnkey,))
            E('v', 'tensor_scalar', rr_col[0:n_p, 0:1], rr_col[0:n_p, 0:1],
              mvb[0:n_p, 1:2], mvb[0:n_p, 1:2], ALU.mult, ALU.mult,
              r=(mvkey, rrkey), w=(rrkey,))
            E('s', 'activation', rr_col[0:n_p, 0:1], rr_col[0:n_p, 0:1], AF.Sqrt,
              bias=eps_rms[0:n_p, 0:1], r=(rrkey, 'ccons'), w=(rrkey,))
            E('v', 'reciprocal', rr_col[0:n_p, 0:1], rr_col[0:n_p, 0:1], r=(rrkey,), w=(rrkey,))

        DMA('L', hid_halo[:], d_hid[0:3, :], w=('hidh',))
        ln_rms(hid_halo, 'hidh', 3, xnh, 'xnh', stath, 'sth', mvh, 'mvh', rrh_c, 'rrh')
        E('s', 'copy', xnh_bf[:], xnh[:], r=('xnh',), w=('xnhb',))
        E('v', 'tensor_scalar', diag3[:], I128f[0:3, 0:3], rrh_c[0:3, 0:1], None,
          ALU.mult, r=('I128f', 'rrh'), w=('diag3',))
        for kt in range(8):
            E('t', 'matmul', PS[0][0:128, 0:3], xnh_bf[0:3, kt * 128:(kt + 1) * 128],
              diag3[:], start=True, stop=True, r=('xnhb', 'diag3'), w=(PK[0],))
            E('v', 'tensor_copy', xr_halo[:, kt * 3:kt * 3 + 3], PS[0][0:128, 0:3],
              r=(PK[0],), w=('xrh',))
        for tg in range(ntg):
            p = tg % 4
            hb = hid_tg[p % 2]
            hkey = f'hid{p % 2}'
            DMA('L', hb[:], d_hid[3 + tg * 128: 3 + (tg + 1) * 128, :], w=(hkey,))
            ln_rms(hb, hkey, 128, xn_tg[p % 2], f'xn{p % 2}', stat[p], f'st{p}', mv[p],
                   f'mv{p}', rr_c[p], f'rr{p}')
            E('s', 'copy', xn_bf[p % 2][:], xn_tg[p % 2][:], r=(f'xn{p % 2}',),
              w=(f'xnb{p % 2}',))
            E('v', 'tensor_scalar', diag[p % 2][:], I128f[:], rr_c[p][:, 0:1], None, ALU.mult,
              r=('I128f', f'rr{p}'), w=(f'diag{p % 2}',))
            for kt in range(8):
                bk = kt // 4
                E('t', 'matmul', PS[bk][0:128, (kt % 4) * 128:(kt % 4) * 128 + 128],
                  xn_bf[p % 2][:, kt * 128:(kt + 1) * 128], diag[p % 2][:],
                  start=True, stop=True, r=(f'xnb{p % 2}', f'diag{p % 2}'), w=(PK[bk],))
            xr_r = xr_fm[:, :].rearrange("q (k t) -> q k t", k=8)
            for bk in range(2):
                E('v', 'tensor_copy',
                  xr_r[:, bk * 4:(bk + 1) * 4, tg * 128:(tg + 1) * 128],
                  PS[bk][0:128, 0:512].rearrange("q (k t) -> q k t", k=4),
                  r=(PK[bk],), w=('xr',))
            DMA('S', d_xnscr[tg * 128:(tg + 1) * 128, :], xn_bf[p % 2][:],
                r=(f'xnb{p % 2}',), w=('xnscr',))
            wzf = Wz_bf[:, :].bitcast(F32)
            slA = wzf[:, 0:1024]
            slB = wzf[:, 1024:2048]
            slAi = Wz_bf[:, :].bitcast(mybir.dt.int32)[:, 0:1024]
            slBi = Wz_bf[:, :].bitcast(mybir.dt.int32)[:, 1024:2048]
            E('v', 'reduce_max', xs_all[:, tg:tg + 1], xn_tg[p % 2][:], axis=AX,
              apply_absolute_value=True, r=(f'xn{p % 2}',), w=('xsall',))
            E('v', 'tensor_scalar', xs_all[:, tg:tg + 1], xs_all[:, tg:tg + 1],
              1e-12, 1.0 / 448.0, ALU.max, ALU.mult, r=('xsall',), w=('xsall',))
            E('v', 'reciprocal', xsr_col[:], xs_all[:, tg:tg + 1], r=('xsall',), w=('xsr',))
            E('s', 'activation', slA, xn_tg[p % 2][:], AF.Abs, scale=xsr_col[:],
              r=(f'xn{p % 2}', 'xsr'), w=('wz',))
            E('v', 'tensor_scalar', slBi, slAi, 23, None, ALU.arith_shift_right,
              r=('wz',), w=('wz',))
            E('v', 'tensor_scalar', slBi, slBi, 121, None, ALU.max, r=('wz',), w=('wz',))
            E('v', 'tensor_scalar', slBi, slBi, -1, 257, ALU.mult, ALU.add,
              r=('wz',), w=('wz',))
            E('v', 'tensor_scalar', slBi, slBi, 23, None, ALU.arith_shift_left,
              r=('wz',), w=('wz',))
            E('v', 'tensor_tensor', slA, slA, slB, ALU.mult, r=('wz',), w=('wz',))
            E('s', 'add', slA, slA, 8388608.0, r=('wz', 'ccons'), w=('wz',))
            E('s', 'add', slA, slA, -8388608.0, r=('wz', 'ccons'), w=('wz',))
            E('v', 'tensor_scalar', slBi, slBi, 23, None, ALU.arith_shift_right,
              r=('wz',), w=('wz',))
            E('v', 'tensor_scalar', slBi, slBi, -1, 254, ALU.mult, ALU.add,
              r=('wz',), w=('wz',))
            E('v', 'tensor_scalar', slBi, slBi, 23, None, ALU.arith_shift_left,
              r=('wz',), w=('wz',))
            E('v', 'tensor_tensor', slA, slA, slB, ALU.mult, r=('wz',), w=('wz',))
            E('s', 'activation', slB, xn_bf[p % 2][:], AF.Sign,
              r=(f'xnb{p % 2}', 'xnscr'), w=('wz',))
            E('v', 'tensor_tensor', xn_bf[p % 2][:], slA, slB, ALU.mult,
              r=('wz',), w=(f'xnb{p % 2}',))
            for kt in range(8):
                bk = kt // 4
                pbq = PS[bk][0:128, 0:512].bitcast(BF16)
                E('t', 'transpose', pbq[:, (kt % 4) * 128:(kt % 4) * 128 + 128],
                  xn_bf[p % 2][:, kt * 128:(kt + 1) * 128], I128b[:],
                  r=(f'xnb{p % 2}', 'I128b'), w=(PK[bk],))
            for bk in range(2):
                E('v', 'tensor_copy', xev[bk][:],
                  PS[bk][0:128, 0:512].bitcast(BF16)[:, 0:512],
                  r=(PK[bk],), w=(f'xev{bk}',))
                DMA('S',
                    d_qscr[bk * 512:(bk + 1) * 512, tg * 128:(tg + 1) * 128]
                    .rearrange("(a q) c -> q a c", q=128),
                    xev[bk][:, :].rearrange("q (a c) -> q a c", a=4),
                    r=(f'xev{bk}',), w=('qscr',))

        def load_w_chunk(bi, kt, c0, width):
            DMA('L', Wst[bi][:, 0:width],
                d_win[kt * 128:(kt + 1) * 128, c0:c0 + width], w=(f'wst{bi}',))
            E('s', 'activation', Wbf[bi][:, 0:width], Wst[bi][:, 0:width], AF.Copy,
              scale=wni[:, kt:kt + 1], r=(f'wst{bi}', 'wni'), w=(f'wbf{bi}',))

        for ct in range(16):
            for kt in range(8):
                bi = kt % 3
                load_w_chunk(bi, kt, 2048 + ct * 128, 128)
                E('t', 'matmul', PS[2][0:128, 0:3], Wbf[bi][:, 0:128],
                  xr_halo[:, kt * 3:kt * 3 + 3], start=(kt == 0), stop=(kt == 7),
                  r=(f'wbf{bi}', 'xrh'), w=(PK[2],))
                for q in range(nq):
                    E('t', 'matmul', PS[3 + q][0:128, 0:512], Wbf[bi][:, 0:128],
                      xr_fm[:, kt * tpc + q * 512: kt * tpc + (q + 1) * 512],
                      start=(kt == 0), stop=(kt == 7),
                      r=(f'wbf{bi}', 'xr'), w=(PK[3 + q],))
            p = ct % 2
            E('v', 'tensor_copy', xpre[p][:, 0:3], PS[2][0:128, 0:3], r=(PK[2],), w=(f'xpre{p}',))
            for q in range(nq):
                E('v', 'tensor_copy', xpre[p][:, 3 + q * 512: 3 + (q + 1) * 512],
                  PS[3 + q][0:128, 0:512], r=(PK[3 + q],), w=(f'xpre{p}',))
            E('v', 'tensor_scalar', cacc[p][:, 0:tpc], xpre[p][:, 0:tpc],
              cw_sb[:, ct * 4:ct * 4 + 1], None, ALU.mult, r=(f'xpre{p}', 'cw'), w=(f'cacc{p}',))
            for j in (1, 2, 3):
                E('v', 'scalar_tensor_tensor', cacc[p][:, 0:tpc], xpre[p][:, j:j + tpc],
                  cw_sb[:, ct * 4 + j:ct * 4 + j + 1], cacc[p][:, 0:tpc],
                  ALU.mult, ALU.add, r=(f'xpre{p}', 'cw', f'cacc{p}'), w=(f'cacc{p}',))
            E('s', 'activation', sx_bf[p][:, 0:tpc], cacc[p][:, 0:tpc], AF.Silu,
              bias=cb_sb[:, ct:ct + 1], r=(f'cacc{p}', 'cb'), w=(f'sx{p}',))
            for g in range(nch):
                bi = (g // 4) % 2
                pb = PS[bi][0:128, 0:512].bitcast(BF16)
                E('t', 'transpose', pb[:, (g % 4) * 128:(g % 4) * 128 + 128],
                  sx_bf[p][:, g * 128:(g + 1) * 128], I128b[:],
                  r=(f'sx{p}', 'I128b'), w=(PK[bi],))
                if g % 4 == 3:
                    E('v', 'tensor_copy', xev[bi][:], pb[:, 0:512], r=(PK[bi],),
                      w=(f'xev{bi}',))
                    DMA('S',
                        d_xscr[(g - 3) * 128:(g + 1) * 128, ct * 128:(ct + 1) * 128]
                        .rearrange("(a q) c -> q a c", q=128),
                        xev[bi][:, :].rearrange("q (a c) -> q a c", a=4),
                        r=(f'xev{bi}',), w=('xscr',))

        for dscr, dkey, base in ((d_bscr, 'bscr', 4096), (d_cscr, 'cscr', 4608)):
            for ct in range(4):
                for kt in range(8):
                    bi = kt % 3
                    load_w_chunk(bi, kt, base + ct * 128, 128)
                    for q in range(nq):
                        E('t', 'matmul', PS[3 + q][0:128, 0:512], Wbf[bi][:, 0:128],
                          xr_fm[:, kt * tpc + q * 512: kt * tpc + (q + 1) * 512],
                          start=(kt == 0), stop=(kt == 7),
                          r=(f'wbf{bi}', 'xr'), w=(PK[3 + q],))
                for q in range(nq):
                    sti = q % 2
                    E('s', 'copy', szst[sti][:], PS[3 + q][0:128, 0:512],
                      r=(PK[3 + q],), w=(f'szst{sti}',))
                    DMA('S', dscr[:, (2 * ct) * tpc + q * 512: (2 * ct) * tpc + (q + 1) * 512],
                        szst[sti][0:64, :], r=(f'szst{sti}',), w=(dkey,))
                    DMA('S', dscr[:, (2 * ct + 1) * tpc + q * 512: (2 * ct + 1) * tpc + (q + 1) * 512],
                        szst[sti][64:128, :], r=(f'szst{sti}',), w=(dkey,))

        for kt in range(8):
            bi = kt % 3
            load_w_chunk(bi, kt, 5120, 8)
            for q in range(nq):
                E('t', 'matmul', PS[3 + q][0:8, 0:512], Wbf[bi][:, 0:8],
                  xr_fm[:, kt * tpc + q * 512: kt * tpc + (q + 1) * 512],
                  start=(kt == 0), stop=(kt == 7),
                  r=(f'wbf{bi}', 'xr'), w=(PK[3 + q],))
        for q in range(nq):
            E('v', 'tensor_copy', dt_raw[:, q * 512:(q + 1) * 512],
              PS[3 + q][0:8, 0:512], r=(PK[3 + q],), w=('dt_raw',))
        for ct in range(16):
            for q in range(nq):
                E('t', 'matmul', PS[0][0:128, 0:512], wdt_sb[:, ct * 128:(ct + 1) * 128],
                  dt_raw[:, q * 512:(q + 1) * 512], start=True, stop=True,
                  r=('wdt', 'dt_raw'), w=(PK[0],))
                E('s', 'activation', spbuf[:, 0:512], PS[0][0:128, 0:512], AF.Exp,
                  bias=bdt_sb[:, ct:ct + 1], r=(PK[0], 'bdt'), w=('spbuf',))
                E('s', 'activation', spbuf[:, 0:512], spbuf[:, 0:512], AF.Ln,
                  bias=one_col[:, 0:1], r=('spbuf', 'ccons'), w=('spbuf',))
                E('t', 'matmul', PS[4 + q][0:8, 0:512],
                  oneh[:, (ct // 2) * 8:(ct // 2) * 8 + 8], spbuf[:, 0:512],
                  start=(ct == 0), stop=(ct == 15),
                  r=('oneh', 'spbuf'), w=(PK[4 + q],))
        for q in range(nq):
            E('v', 'tensor_scalar', dtA_fm[:, q * 512:(q + 1) * 512], PS[4 + q][0:8, 0:512],
              negA[0:8, 0:1], None, ALU.mult, r=(PK[4 + q], 'negA'), w=('dtA',))

        for zc in range(4):
            for kt in range(8):
                bi = kt % 3
                DMA('L', Wst[bi][:, 0:512],
                    d_win[kt * 128:(kt + 1) * 128, zc * 512:(zc + 1) * 512], w=(f'wst{bi}',))
                E('s', 'activation', Wz_bf[:, kt * 512:(kt + 1) * 512], Wst[bi][:, 0:512],
                  AF.Copy, scale=wni[:, kt:kt + 1], r=(f'wst{bi}', 'wni'), w=('wz',))
            for tg in range(ntg):
                for kt in range(8):
                    E('t', 'matmul', PS[tg % 2][0:128, 0:512],
                      xr_fm[:, kt * tpc + tg * 128:kt * tpc + (tg + 1) * 128],
                      Wz_bf[:, kt * 512:(kt + 1) * 512],
                      start=(kt == 0), stop=(kt == 7),
                      r=('xr', 'wz'), w=(PK[tg % 2],))
                bi = tg % 2
                E('s', 'activation', szst[bi][:], PS[tg % 2][0:128, 0:512], AF.Silu,
                  r=(PK[tg % 2],), w=(f'szst{bi}',))
                DMA('S', d_zscr[tg * 128:(tg + 1) * 128, zc * 512:(zc + 1) * 512],
                    szst[bi][:], r=(f'szst{bi}',), w=('zscr',))

        freed_keys = ('xr', 'xrh', 'wdt', 'oneh', 'hidh', 'xnh', 'xnhb', 'sth',
                      'mvh', 'rrh', 'diag3', 'wz', 'xpre0', 'xpre1', 'cacc0', 'cacc1',
                      'sx0', 'sx1', 'xev0', 'xev1', 'spbuf', 'szst0', 'szst1', 'dt_raw')
        pre('y', freed_keys, freed_keys)
        for g_ in reversed(groupA):
            g_.__exit__(None, None, None)
        groupB = []

        def SBB(nm, shp, dt=F32):
            g = nc.sbuf_tensor(nm, shp, dt)
            t = g.__enter__()
            groupB.append(g)
            return t

        Hg = SBB("Hg", [64, DI])
        Wout_bf = SBB("Wout_bf", [128, 16 * 1024], BF16)
        WffnT = SBB("WffnT", [128, 8 * 1024], BF16)
        wu8 = SBB("wu8", [128, 1024], mybir.dt.uint8)
        wf8b = SBB("wf8b", [128, 1024], BF16)
        db1 = SBB("db1", [128, 1024]); db2 = SBB("db2", [128, 1024])
        db3 = SBB("db3", [128, 1024]); db4 = SBB("db4", [128, 1024])
        dbi = SBB("dbi", [128, 1024], mybir.dt.int32)
        sz_c = [SBB(f"sz_c{i}", [128, DI], BF16) for i in range(2)]
        yg_sb = [SBB(f"yg_sb{i}", [128, DI], BF16) for i in range(2)]
        yn_fm = [SBB(f"yn_fm{i}", [128, DI], BF16) for i in range(2)]
        out_buf = [SBB(f"out_buf{i}", [128, D]) for i in range(2)]
        xnl = [SBB(f"xnl{i}", [128, D], BF16) for i in range(2)]
        B_tm4 = [SBB(f"B_tm4_{i}", [128, 64], BF16) for i in range(4)]
        Sbf4 = [SBB(f"Sbf4_{i}", [128, 128], BF16) for i in range(4)]
        wx4 = [SBB(f"wx4_{i}", [128, HD], BF16) for i in range(4)]
        yb4 = [SBB(f"yb4_{i}", [128, HD]) for i in range(4)]
        q_fm = [SBB(f"q_fm{i}", [128, D], BF16) for i in range(2)]

        for ct in range(16):
            for half in range(2):
                bi = half
                DMA('L', Wst[bi][:, 0:512],
                    d_wout[ct * 128:(ct + 1) * 128, half * 512:(half + 1) * 512],
                    w=(f'wst{bi}',))
                E('s', 'activation',
                  Wout_bf[:, ct * 1024 + half * 512: ct * 1024 + (half + 1) * 512],
                  Wst[bi][:, 0:512], AF.Copy, scale=wno[:, ct:ct + 1],
                  r=(f'wst{bi}', 'wno'), w=('wout',))
        for ot in range(8):
            DMA('L', wu8[:], d_wffn[ot * 128:(ot + 1) * 128, :], w=('wf8',))
            E('s', 'copy', db1[:], wu8[:], r=('wf8',), w=('db1',))
            E('v', 'tensor_scalar', db2[:], db1[:], 127.5, None, ALU.is_ge,
              r=('db1',), w=('db2',))
            E('v', 'scalar_tensor_tensor', db3[:], db2[:], -128.0, db1[:],
              ALU.mult, ALU.add, r=('db1', 'db2'), w=('db3',))
            E('v', 'tensor_scalar', db2[:], db2[:], -2.0, 1.0, ALU.mult, ALU.add,
              r=('db2',), w=('db2',))
            E('v', 'tensor_scalar', db4[:], db3[:], 0.125, -0.4375, ALU.mult, ALU.add,
              r=('db3',), w=('db4',))
            E('s', 'add', db4[:], db4[:], 8388608.0, r=('db4', 'ccons'), w=('db4',))
            E('s', 'add', db4[:], db4[:], -8388608.0, r=('db4', 'ccons'), w=('db4',))
            E('v', 'scalar_tensor_tensor', db4[:], db4[:], -8.0, db3[:],
              ALU.mult, ALU.add, r=('db3', 'db4'), w=('db4',))
            E('v', 'scalar_tensor_tensor', db3[:], db4[:], -1.0, db3[:],
              ALU.mult, ALU.add, r=('db3', 'db4'), w=('db3',))
            E('v', 'tensor_scalar', db3[:], db3[:], 0.125, None, ALU.mult,
              r=('db3',), w=('db3',))
            E('v', 'tensor_scalar', db1[:], db3[:], 8.0, 960.0,
              ALU.mult, ALU.add, r=('db3',), w=('db1',))
            E('v', 'tensor_tensor', db1[:], db1[:], db4[:], ALU.add,
              r=('db1', 'db4'), w=('db1',))
            E('s', 'copy', dbi[:], db1[:], r=('db1',), w=('dbi',))
            E('v', 'tensor_scalar', dbi[:], dbi[:], 20, None, ALU.arith_shift_left,
              r=('dbi',), w=('dbi',))
            E('v', 'tensor_scalar', db3[:], db3[:], 0.5, None, ALU.is_ge,
              r=('db3',), w=('db3',))
            E('v', 'tensor_scalar', db4[:], db4[:], 0.001953125, None, ALU.mult,
              r=('db4',), w=('db4',))
            E('v', 'tensor_tensor', db1[:], dbi[:, :].bitcast(F32),
              db4[:], ALU.subtract, r=('dbi', 'db4'), w=('db1',))
            E('v', 'tensor_tensor', db1[:], db1[:], db3[:], ALU.mult,
              r=('db1', 'db3'), w=('db1',))
            E('v', 'tensor_tensor', db1[:], db1[:], db4[:],
              ALU.add, r=('db1', 'db4'), w=('db1',))
            E('v', 'tensor_tensor', wf8b[:], db1[:], db2[:], ALU.mult,
              r=('db1', 'db2'), w=('wf8b',))
            for kt in range(8):
                E('t', 'transpose', PSB[kt % 2], wf8b[:, kt * 128:(kt + 1) * 128], I128b[:],
                  r=('wf8b', 'I128b'), w=(PK[kt % 2],))
                E('s', 'copy', WffnT[:, kt * 1024 + ot * 128: kt * 1024 + (ot + 1) * 128],
                  PSB[kt % 2], r=(PK[kt % 2],), w=('wffnT',))

        bscr_r = d_bscr.rearrange("p (h t) -> p h t", h=8)
        cscr_r = d_cscr.rearrange("p (h t) -> p h t", h=8)
        Bc_r = [b_[:, :].rearrange("p (h t) -> p h t", h=8) for b_ in Bc_b]
        Cc_r = [c_[:, :].rearrange("p (h t) -> p h t", h=8) for c_ in Cc_b]
        HKEYS = tuple(f'hsb{h}' for h in range(NH))
        E('v', 'memset', h_sb[:], 0.0, w=HKEYS)
        DMA('L', x_c[0][:], d_xscr[0:128, :], r=('xscr',), w=('xc0',))
        DMA('L', Bc_r[0], bscr_r[:, :, 0:128], r=('bscr',), w=('Bc0',))
        for c in range(nch):
            xcb = x_c[c % 2]
            xckey = f'xc{c % 2}'
            if c + 1 < nch:
                DMA('L', x_c[(c + 1) % 2][:], d_xscr[(c + 1) * 128:(c + 2) * 128, :],
                    r=('xscr',), w=(f'xc{(c + 1) % 2}',))
                DMA('L', Bc_r[(c + 1) % 2], bscr_r[:, :, (c + 1) * 128:(c + 2) * 128],
                    r=('bscr',), w=(f'Bc{(c + 1) % 2}',))
            E('t', 'transpose', PS[0][0:128, (c % 2) * 8:(c % 2) * 8 + 8],
              dtA_fm[:, c * 128:(c + 1) * 128],
              I128f[0:8, 0:8], r=('dtA', 'I128f'), w=(PK[0],))
            E('v', 'tensor_copy', dtA_tm[c % 2][:], PS[0][0:128, (c % 2) * 8:(c % 2) * 8 + 8],
              r=(PK[0],), w=(f'dtA_tm{c % 2}',))
            E('t', 'matmul', PS[1][0:128, (c % 2) * 8:(c % 2) * 8 + 8], U128[:], dtA_tm[c % 2][:],
              start=True, stop=True, r=('U128', f'dtA_tm{c % 2}'), w=(PK[1],))
            E('s', 'activation', explcs[:, c * 8:(c + 1) * 8],
              PS[1][0:128, (c % 2) * 8:(c % 2) * 8 + 8], AF.Exp,
              r=(PK[1],), w=('explcs',))
            E('v', 'tensor_scalar', expneg[:, c * 8:(c + 1) * 8],
              PS[1][0:128, (c % 2) * 8:(c % 2) * 8 + 8],
              -1.0, None, ALU.mult, r=(PK[1],), w=('expneg',))
            E('s', 'activation', expneg[:, c * 8:(c + 1) * 8], expneg[:, c * 8:(c + 1) * 8],
              AF.Exp, r=('expneg',), w=('expneg',))
            E('s', 'copy', expneg_bf[:, c * 8:(c + 1) * 8], expneg[:, c * 8:(c + 1) * 8],
              r=('expneg',), w=('expneg_bf',))
            E('t', 'matmul', PS[2][0:1, 0:8], e127_col, explcs[:, c * 8:(c + 1) * 8],
              start=True, stop=True, r=('ccons', 'explcs'), w=(PK[2],))
            E('v', 'tensor_copy', el_last[:], PS[2][0:1, 0:8], r=(PK[2],), w=('el_last',))
            E('t', 'matmul', PS[2][0:64, 0:8], ones1[0:1, 0:64], el_last[:],
              start=True, stop=True, r=('ones1', 'el_last'), w=(PK[2],))
            E('v', 'tensor_copy', tdrow[:, c * 8:(c + 1) * 8], PS[2][0:64, 0:8],
              r=(PK[2],), w=('tdrow',))
            PSB3 = PS[3][0:128, 0:512].bitcast(BF16)
            for h in range(NH):
                E('t', 'transpose', PSB3[:, h * 64:h * 64 + 64],
                  Bc_b[c % 2][:, h * 128:(h + 1) * 128], I128b[0:64, 0:64],
                  r=(f'Bc{c % 2}', 'I128b'), w=(PK[3],))
            for g in range(2):
                hs = [4 * g + i for i in range(4)]
                for hh, h in enumerate(hs):
                    E('s', 'copy', B_tm4[hh][:], PSB3[:, h * 64:h * 64 + 64],
                      r=(PK[3],), w=(f'Bt{hh}',))
                for hh, h in enumerate(hs):
                    E('v', 'tensor_scalar', wx4[hh][:], xcb[:, h * HD:(h + 1) * HD],
                      expneg[:, c * 8 + h:c * 8 + h + 1], None, ALU.mult,
                      r=(xckey, 'expneg'), w=(f'wx{hh}',))
                for hh, h in enumerate(hs):
                    ps = PS[4 + 2 * g + hh // 2][0:64, (hh % 2) * HD:(hh % 2) * HD + HD]
                    E('t', 'matmul', ps, B_tm4[hh][:, 0:64], wx4[hh][:],
                      start=True, stop=True, r=(f'Bt{hh}', f'wx{hh}'),
                      w=(PK[4 + 2 * g + hh // 2],))
                for hh, h in enumerate(hs):
                    ps = PS[4 + 2 * g + hh // 2][0:64, (hh % 2) * HD:(hh % 2) * HD + HD]
                    hhs = h_sb[:, h * HD:(h + 1) * HD]
                    E('v', 'tensor_tensor', hhs, hhs, ps, ALU.add,
                      r=(f'hsb{h}', PK[4 + 2 * g + hh // 2]), w=(f'hsb{h}',))
                    E('v', 'tensor_scalar', hhs, hhs, tdrow[:, c * 8 + h:c * 8 + h + 1],
                      None, ALU.mult, r=(f'hsb{h}', 'tdrow'), w=(f'hsb{h}',))
        E('v', 'memset', Drow[:], 1.0, w=('Drow',))
        for c in range(nch):
            E('v', 'tensor_tensor', Drow[:], Drow[:], tdrow[0:1, c * 8:(c + 1) * 8],
              ALU.mult, r=('Drow', 'tdrow'), w=('Drow',))

        DMA('L', d_ccin[0:HSZ].rearrange("(p f) -> p f", p=64), h_sb[:],
            r=HKEYS, w=('ccin',))
        DMA('L', d_ccin[HSZ:HSZ + 8].rearrange("(a b) -> a b", a=1), Drow[:],
            r=('Drow',), w=('ccin',))
        pre('g', ('ccin',), ('ccout',))
        cnt['g'] += 1
        ccmy = cnt['g']

        def emit_cc(engobj):
            ins = engobj.collective_compute(
                "AllGather", ALU.bypass,
                replica_groups=[[0, 1, 2, 3], [4, 5, 6, 7]],
                ins=[d_ccin], outs=[d_ccout])
            ins.then_inc(sem['g'], 1)
        progs['g'].append(emit_cc)
        last_w['ccout'] = ('g', ccmy)
        readers['ccout'] = {}
        for j in range(4):
            DMA('L', Dg[:, j * 8:(j + 1) * 8], d_ccout[j:j + 1, HSZ:HSZ + 8],
                r=('ccout',), w=('Dga',))
        for j in range(4):
            E('v', 'tensor_scalar', Dp[:, j * 8:(j + 1) * 8], Dg[:, j * 8:(j + 1) * 8],
              1.0, smask_sb[0:1, j:j + 1], ALU.subtract, ALU.mult,
              r=('Dga', 'smask'), w=('Dpa',))
            E('v', 'tensor_scalar', Dp[:, j * 8:(j + 1) * 8], Dp[:, j * 8:(j + 1) * 8],
              1.0, None, ALU.add, r=('Dpa',), w=('Dpa',))
        E('v', 'memset', h_sb[:], 0.0, w=HKEYS)
        E('v', 'memset', coef[:], 1.0, w=('coef',))
        for j in (3, 2, 1, 0):
            DMA('L', Hg[:], d_ccout[j:j + 1, 0:HSZ].rearrange("a (p f) -> (a p) f", p=64),
                r=('ccout',), w=('Hga',))
            E('v', 'tensor_scalar', cj_row[:], coef[:], smask_sb[0:1, j:j + 1], None,
              ALU.mult, r=('coef', 'smask'), w=('cj',))
            E('t', 'matmul', PS[0][0:64, 0:8], ones1[0:1, 0:64], cj_row[:],
              start=True, stop=True, r=('ones1', 'cj'), w=(PK[0],))
            E('v', 'tensor_copy', cjt[:], PS[0][0:64, 0:8], r=(PK[0],), w=('cjt',))
            for h in range(NH):
                E('v', 'scalar_tensor_tensor', h_sb[:, h * HD:(h + 1) * HD],
                  Hg[:, h * HD:(h + 1) * HD], cjt[:, h:h + 1],
                  h_sb[:, h * HD:(h + 1) * HD], ALU.mult, ALU.add,
                  r=('Hga', 'cjt', f'hsb{h}'), w=(f'hsb{h}',))
            E('v', 'tensor_tensor', coef[:], coef[:], Dp[:, j * 8:(j + 1) * 8], ALU.mult,
              r=('coef', 'Dpa'), w=('coef',))

        if debug:
            DMA('S', d_dbg3[0:64, :], h_sb[:], r=HKEYS)

        DMA('L', sz_c[0][:], d_zscr[0:128, :], r=('zscr',), w=('sz0',))
        DMA('L', x_c[0][:], d_xscr[0:128, :], r=('xscr',), w=('xc0',))
        DMA('L', Bc_r[0], bscr_r[:, :, 0:128], r=('bscr',), w=('Bc0',))
        DMA('L', Cc_r[0], cscr_r[:, :, 0:128], r=('cscr',), w=('Cc0',))

        def scan_emit(c):
            cp = c % 2
            szb = sz_c[cp]
            szkey = f'sz{cp}'
            xcb = x_c[cp]
            xckey = f'xc{cp}'
            bkey = f'Bc{cp}'
            ckey = f'Cc{cp}'
            if c + 1 < nch:
                np1 = (c + 1) % 2
                DMA('L', sz_c[np1][:], d_zscr[(c + 1) * 128:(c + 2) * 128, :],
                    r=('zscr',), w=(f'sz{np1}',))
                DMA('L', x_c[np1][:], d_xscr[(c + 1) * 128:(c + 2) * 128, :],
                    r=('xscr',), w=(f'xc{np1}',))
                DMA('L', Bc_r[np1], bscr_r[:, :, (c + 1) * 128:(c + 2) * 128],
                    r=('bscr',), w=(f'Bc{np1}',))
                DMA('L', Cc_r[np1], cscr_r[:, :, (c + 1) * 128:(c + 2) * 128],
                    r=('cscr',), w=(f'Cc{np1}',))
            E('s', 'copy', h_bf[:], h_sb[:], r=HKEYS, w=('h_bf',))
            for g in range(2):
                hs = [4 * g + i for i in range(4)]
                for hh, h in enumerate(hs):
                    bsl = Bc_b[cp][:, h * 128:(h + 1) * 128]
                    csl = Cc_b[cp][:, h * 128:(h + 1) * 128]
                    E('t', 'matmul', PS[0][0:128, hh * 128:(hh + 1) * 128], bsl, csl,
                      start=True, stop=True, r=(bkey, ckey), w=(PK[0],))
                for hh, h in enumerate(hs):
                    E('v', 'tensor_scalar', wx4[hh][:],
                      xcb[:, h * HD:(h + 1) * HD],
                      expneg[:, c * 8 + h:c * 8 + h + 1], None, ALU.mult,
                      r=(xckey, 'expneg'), w=(f'wx{hh}',))
                for hh, h in enumerate(hs):
                    E('v', 'tensor_tensor', Sbf4[hh][:],
                      PS[0][0:128, hh * 128:(hh + 1) * 128], U128[:], ALU.mult,
                      r=(PK[0], 'U128'), w=(f'Sb{hh}',))
                for hh, h in enumerate(hs):
                    csl = Cc_b[cp][:, h * 128:(h + 1) * 128]
                    yb = PS[1 + hh // 2][0:128, (hh % 2) * HD:(hh % 2) * HD + HD]
                    E('t', 'matmul', yb, Sbf4[hh][:], wx4[hh][:], start=True, stop=False,
                      r=(f'Sb{hh}', f'wx{hh}'), w=(PK[1 + hh // 2],))
                    E('t', 'matmul', yb, csl, h_bf[:, h * HD:(h + 1) * HD],
                      start=False, stop=True, r=(ckey, 'h_bf'), w=(PK[1 + hh // 2],))
                PSB0 = PS[0][0:128, 0:512].bitcast(BF16)
                for hh, h in enumerate(hs):
                    bsl = Bc_b[cp][:, h * 128:(h + 1) * 128]
                    E('t', 'transpose', PSB0[:, hh * 128:hh * 128 + 64], bsl,
                      I128b[0:64, 0:64], r=(bkey, 'I128b'), w=(PK[0],))
                for hh, h in enumerate(hs):
                    E('s', 'copy', B_tm4[hh][:], PSB0[:, hh * 128:hh * 128 + 64],
                      r=(PK[0],), w=(f'Bt{hh}',))
                for hh, h in enumerate(hs):
                    ps = PS[3 + hh // 2][0:64, (hh % 2) * HD:(hh % 2) * HD + HD]
                    E('t', 'matmul', ps, B_tm4[hh][:, 0:64], wx4[hh][:],
                      start=True, stop=True, r=(f'Bt{hh}', f'wx{hh}'), w=(PK[3 + hh // 2],))
                for hh, h in enumerate(hs):
                    yb = PS[1 + hh // 2][0:128, (hh % 2) * HD:(hh % 2) * HD + HD]
                    elcol = explcs[:, c * 8 + h:c * 8 + h + 1]
                    E('v', 'tensor_scalar', yb4[hh][:], yb, elcol, None, ALU.mult,
                      r=(PK[1 + hh // 2], 'explcs'), w=(f'yb{hh}',))
                    E('v', 'scalar_tensor_tensor', yb4[hh][:],
                      xcb[:, h * HD:(h + 1) * HD], Dbc[:, h:h + 1], yb4[hh][:],
                      ALU.mult, ALU.add, r=(xckey, 'Dbc', f'yb{hh}'), w=(f'yb{hh}',))
                    E('v', 'tensor_tensor', yg_sb[cp][:, h * HD:(h + 1) * HD], yb4[hh][:],
                      szb[:, h * HD:(h + 1) * HD], ALU.mult, r=(f'yb{hh}', szkey),
                      w=(f'yg{cp}_{h}',))
                for hh, h in enumerate(hs):
                    ps = PS[3 + hh // 2][0:64, (hh % 2) * HD:(hh % 2) * HD + HD]
                    hhs = h_sb[:, h * HD:(h + 1) * HD]
                    E('v', 'tensor_tensor', hhs, hhs, ps, ALU.add,
                      r=(f'hsb{h}', PK[3 + hh // 2]), w=(f'hsb{h}',))
                    E('v', 'tensor_scalar', hhs, hhs, tdrow[:, c * 8 + h:c * 8 + h + 1],
                      None, ALU.mult, r=(f'hsb{h}', 'tdrow'), w=(f'hsb{h}',))

        def epi_emit(c):
            cp = c % 2
            YGK = tuple(f'yg{cp}_{h}' for h in range(NH))
            if debug:
                DMA('S', d_dbg[c * 128:(c + 1) * 128, :], yg_sb[cp][:], r=YGK)
            for i in range(4):
                E('v', 'bn_stats', stat[cp][:, 6 * i:6 * i + 6],
                  yg_sb[cp][:, 512 * i:512 * (i + 1)], r=YGK, w=(f'st{cp}',))
            E('v', 'bn_aggr', mv[cp][:, 0:2], stat[cp][:, 0:24], r=(f'st{cp}',), w=(f'mv{cp}',))
            E('v', 'tensor_scalar', rry_c[:], mv[cp][:, 0:1], mv[cp][:, 0:1], mv[cp][:, 1:2],
              ALU.mult, ALU.add, r=(f'mv{cp}',), w=('rry',))
            E('s', 'activation', rry_c[:], rry_c[:], AF.Sqrt,
              bias=eps_rms[:, 0:1], r=('rry', 'ccons'), w=('rry',))
            E('v', 'reciprocal', rry_c[:], rry_c[:], r=('rry',), w=('rry',))
            E('v', 'tensor_scalar', diag[cp][:], I128f[:], rry_c[:, 0:1], None, ALU.mult,
              r=('I128f', 'rry'), w=(f'diag{cp}',))
            yn_r = yn_fm[cp][:, :].rearrange("q (a t) -> q a t", a=16)
            for ct in range(16):
                E('t', 'matmul', PS[5][0:128, (ct % 4) * 128:(ct % 4) * 128 + 128],
                  yg_sb[cp][:, ct * 128:(ct + 1) * 128], diag[cp][:], start=True, stop=True,
                  r=YGK + (f'diag{cp}',), w=(PK[5],))
                if ct % 4 == 3:
                    E('s', 'copy', yn_r[:, ct - 3:ct + 1, :],
                      PS[5][0:128, 0:512].rearrange("q (a t) -> q a t", a=4),
                      r=(PK[5],), w=(f'yn{cp}',))
            hb = hid_tg[cp]
            hkey = f'hid{cp}'
            DMA('L', hb[:], d_hid[3 + c * 128: 3 + (c + 1) * 128, :], w=(hkey,))
            DMA('L', xnl[cp][:], d_xnscr[c * 128:(c + 1) * 128, :],
                r=('xnscr',), w=(f'xnl{cp}',))
            DMA('L',
                q_fm[cp][:, :].rearrange("q (k c) -> q k c", k=8),
                d_qscr[:, c * 128:(c + 1) * 128].rearrange("(k q) c -> q k c", q=128),
                r=('qscr',), w=(f'qfm{cp}',))
            for half in range(2):
                for kt in range(8):
                    E('t', 'matmul', PS[6 + half][0:128, 0:512],
                      q_fm[cp][:, kt * 128:(kt + 1) * 128],
                      WffnT[:, kt * 1024 + half * 512: kt * 1024 + (half + 1) * 512],
                      start=(kt == 0), stop=(kt == 7),
                      r=(f'qfm{cp}', 'wffnT'), w=(PK[6 + half],))
            E('v', 'tensor_tensor', xsw[:], xs_all[:, c:c + 1], wscbc[:], ALU.mult,
              r=('xsall', 'wscbc'), w=('xsw',))
            for half in range(2):
                sl = slice(half * 512, (half + 1) * 512)
                E('v', 'scalar_tensor_tensor', db4[:, sl], PS[6 + half][0:128, 0:512],
                  xsw[:], biasbc[:, sl], ALU.mult, ALU.add,
                  r=(PK[6 + half], 'xsw', 'biasbc'), w=('db4',))
                E('s', 'activation', db2[:, sl], db4[:, sl], AF.Silu, r=('db4',), w=('db2',))
            for half in range(2):
                for ct in range(16):
                    E('t', 'matmul', PS[6 + half][0:128, 0:512],
                      yn_fm[cp][:, ct * 128:(ct + 1) * 128],
                      Wout_bf[:, ct * 1024 + half * 512: ct * 1024 + (half + 1) * 512],
                      start=(ct == 0), stop=(ct == 15),
                      r=(f'yn{cp}', 'wout'), w=(PK[6 + half],))
            ob = out_buf[cp]
            okey = f'ob{cp}'
            for half in range(2):
                sl = slice(half * 512, (half + 1) * 512)
                E('v', 'tensor_tensor', db3[:, sl], PS[6 + half][0:128, 0:512],
                  xnl[cp][:, sl], ALU.add, r=(PK[6 + half], f'xnl{cp}'), w=('db3',))
                E('v', 'tensor_tensor', db4[:, sl], db3[:, sl], db2[:, sl], ALU.mult,
                  r=('db3', 'db2'), w=('db4',))
                E('v', 'tensor_tensor', ob[:, sl], db4[:, sl], hb[:, sl], ALU.add,
                  r=('db4', hkey), w=(okey,))
            DMA('S', d_out[c * 128:(c + 1) * 128, :], ob[:], r=(okey,))
            if debug:
                DMA('S', d_dbg2[c * 128:(c + 1) * 128, :], yn_fm[cp][:], r=(f'yn{cp}',))
                DMA('S', d_dbg4[c * 128:(c + 1) * 128, :], db3[:], r=('db3',))
                DMA('S', d_dbg5[c * 128:(c + 1) * 128, :], db2[:], r=('db2',))

        for c in range(nch):
            scan_emit(c)
            if c >= 1:
                epi_emit(c - 1)
        epi_emit(nch - 1)

        for lab in QSEMS['S'] + QSEMS['L']:
            _wait('y', lab, cnt[lab])

        for g_ in reversed(groupB):
            g_.__exit__(None, None, None)

        with nc.Block() as blk:
            blk.sync(lambda e: [f(e) for f in progs['y']])
            blk.vector(lambda e: [f(e) for f in progs['v']])
            blk.scalar(lambda e: [f(e) for f in progs['s']])
            blk.tensor(lambda e: [f(e) for f in progs['t']])
            blk.gpsimd(lambda e: [f(e) for f in progs['g']])

    return nc


_NC_CACHE = {}


def _get_nc(tpc):
    if tpc not in _NC_CACHE:
        _NC_CACHE[tpc] = build_program(tpc)
    return _NC_CACHE[tpc]


def make_consts():
    I128f = np.eye(128, dtype=np.float32)
    I128b = np.eye(128, dtype=ml_dtypes.bfloat16)
    U128 = np.triu(np.ones((128, 128), dtype=np.float32))
    oneh = np.zeros((128, 64), dtype=np.float32)
    for j in range(8):
        oneh[:, j * 8 + j] = 1.0
    ones1 = np.ones((1, 128), dtype=np.float32)
    ccons = np.zeros((128, 6), dtype=np.float32)
    ccons[:, 0] = LN_EPS
    ccons[:, 1] = RMS_EPS
    ccons[:, 2] = 1.0
    ccons[127, 3] = 1.0
    ccons[:, 4] = 8388608.0
    ccons[:, 5] = -8388608.0
    icons_np = np.zeros((128, 4), dtype=np.int32)
    icons_np[:, 0] = 8388608
    icons_np[:, 1] = 1006632960
    icons_np[:, 2] = 1048576
    icons_np[:, 3] = -25165824
    return I128f, I128b, U128, oneh, ones1, ccons, icons_np


def make_in_maps(tpc, hidden, wni, win, cw, cb, wdt, bdt, alog, dh, wno, wout,
                 wffn, wsc, fbias):
    I128f, I128b, U128, oneh, ones1, ccons, icons_np = make_consts()
    in_maps = []
    for core in range(8):
        b, s = core // 4, core % 4
        lo = s * tpc - 3
        if lo < 0:
            hid_k = np.concatenate(
                [np.zeros((3, D), np.float32), hidden[b, 0:(s + 1) * tpc]], axis=0)
        else:
            hid_k = hidden[b, lo:(s + 1) * tpc]
        smask = np.array([[1.0 if j < s else 0.0 for j in range(4)]], np.float32)
        in_maps.append({
            "hid": np.ascontiguousarray(hid_k, np.float32),
            "w_norm_in": np.ascontiguousarray(np.asarray(wni, np.float32).reshape(8, 128).T),
            "w_in_proj": np.ascontiguousarray(win, np.float32),
            "conv_w": np.ascontiguousarray(
                np.asarray(cw, np.float32).reshape(16, 128, 4).transpose(1, 0, 2).reshape(128, 64)),
            "conv_b": np.ascontiguousarray(np.asarray(cb, np.float32).reshape(16, 128).T),
            "w_dt": np.ascontiguousarray(wdt, np.float32),
            "b_dt": np.ascontiguousarray(np.asarray(bdt, np.float32).reshape(16, 128).T),
            "A_log": np.ascontiguousarray(np.asarray(alog, np.float32).reshape(NH, 1)),
            "D_head": np.ascontiguousarray(np.asarray(dh, np.float32).reshape(1, NH)),
            "w_norm_out": np.ascontiguousarray(np.asarray(wno, np.float32).reshape(16, 128).T),
            "w_out_proj": np.ascontiguousarray(wout, np.float32),
            "ffn_w": np.ascontiguousarray(np.asarray(wffn).view(np.uint8)),
            "ffn_wscale": np.ascontiguousarray(np.asarray(wsc, np.float32).reshape(1, 1)),
            "ffn_bias": np.ascontiguousarray(np.asarray(fbias, np.float32).reshape(1, D)),
            "smask": smask,
            "I128f": I128f, "I128b": I128b, "U128": U128, "oneh": oneh,
            "ones1": ones1, "colconsts": ccons, "intconsts": icons_np,
        })
    return in_maps


def kernel(L0_attn_hidden, L0_attn_w_norm_in, L0_attn_w_in_proj, L0_attn_conv_w,
           L0_attn_conv_b, L0_attn_w_dt, L0_attn_b_dt, L0_attn_A_log, L0_attn_D,
           L0_attn_w_norm_out, L0_attn_w_out_proj, L0_ffn_weight, L0_ffn_weight_scale,
           L0_ffn_bias):
    hidden = np.asarray(L0_attn_hidden, np.float32)
    Bz, S, _ = hidden.shape
    tpc = S // 4
    nc = _get_nc(tpc)
    in_maps = make_in_maps(
        tpc, hidden, L0_attn_w_norm_in, L0_attn_w_in_proj, L0_attn_conv_w,
        L0_attn_conv_b, L0_attn_w_dt, L0_attn_b_dt, L0_attn_A_log, L0_attn_D,
        L0_attn_w_norm_out, L0_attn_w_out_proj, L0_ffn_weight, L0_ffn_weight_scale,
        L0_ffn_bias)
    res = run_bass_kernel_spmd(nc, in_maps, core_ids=list(range(8)))
    out = np.empty((Bz, S, D), np.float32)
    for core in range(8):
        b, s = core // 4, core % 4
        out[b, s * tpc:(s + 1) * tpc] = res.results[core]["out"]
    return out


# version 14
# speedup vs baseline: 1.0122x; 1.0068x over previous
import numpy as np
from contextlib import ExitStack
import concourse.bass as bass
import concourse.mybir as mybir
from concourse.bass_utils import run_bass_kernel_spmd
import ml_dtypes

F32 = mybir.dt.float32
BF16 = mybir.dt.bfloat16
FP8 = mybir.dt.float8e4
AF = mybir.ActivationFunctionType
ALU = mybir.AluOpType
AX = mybir.AxisListType.X

D = 1024
DI = 2048
NH = 8
HD = 256
NST = 64
CH = 128
PROJ = 5128
LN_EPS = 1e-5
RMS_EPS = float(np.finfo(np.float32).eps)


def build_program(tpc, debug=False):
    nch = tpc // CH
    ntg = tpc // 128
    nq = max(1, tpc // 512)
    HSZ = NST * DI
    nc = bass.Bass("TRN2", target_bir_lowering=False)

    d_hid = nc.dram_tensor("hid", [tpc + 3, D], F32, kind="ExternalInput").ap()
    d_wni = nc.dram_tensor("w_norm_in", [128, 8], F32, kind="ExternalInput").ap()
    d_win = nc.dram_tensor("w_in_proj", [D, PROJ], F32, kind="ExternalInput").ap()
    d_cw = nc.dram_tensor("conv_w", [128, 64], F32, kind="ExternalInput").ap()
    d_cb = nc.dram_tensor("conv_b", [128, 16], F32, kind="ExternalInput").ap()
    d_wdt = nc.dram_tensor("w_dt", [NH, DI], F32, kind="ExternalInput").ap()
    d_bdt = nc.dram_tensor("b_dt", [128, 16], F32, kind="ExternalInput").ap()
    d_alog = nc.dram_tensor("A_log", [NH, 1], F32, kind="ExternalInput").ap()
    d_Dh = nc.dram_tensor("D_head", [1, NH], F32, kind="ExternalInput").ap()
    d_wno = nc.dram_tensor("w_norm_out", [128, 16], F32, kind="ExternalInput").ap()
    d_wout = nc.dram_tensor("w_out_proj", [DI, D], F32, kind="ExternalInput").ap()
    d_wffn = nc.dram_tensor("ffn_w", [D, D], mybir.dt.uint8, kind="ExternalInput").ap()
    d_wsc = nc.dram_tensor("ffn_wscale", [1, 1], F32, kind="ExternalInput").ap()
    d_fb = nc.dram_tensor("ffn_bias", [1, D], F32, kind="ExternalInput").ap()
    d_smask = nc.dram_tensor("smask", [1, 4], F32, kind="ExternalInput").ap()
    d_I128f = nc.dram_tensor("I128f", [128, 128], F32, kind="ExternalInput").ap()
    d_I128b = nc.dram_tensor("I128b", [128, 128], BF16, kind="ExternalInput").ap()
    d_U128 = nc.dram_tensor("U128", [128, 128], F32, kind="ExternalInput").ap()
    d_oneh = nc.dram_tensor("oneh", [128, 64], F32, kind="ExternalInput").ap()
    d_ones = nc.dram_tensor("ones1", [1, 128], F32, kind="ExternalInput").ap()
    d_ccons = nc.dram_tensor("colconsts", [128, 6], F32, kind="ExternalInput").ap()
    d_icons = nc.dram_tensor("intconsts", [128, 4], mybir.dt.int32, kind="ExternalInput").ap()
    d_out = nc.dram_tensor("out", [tpc, D], F32, kind="ExternalOutput").ap()
    d_zscr = nc.dram_tensor("zscr", [tpc, DI], BF16, kind="Internal").ap()
    d_xscr = nc.dram_tensor("xscr", [tpc, DI], BF16, kind="Internal").ap()
    d_qscr = nc.dram_tensor("qscr", [D, tpc], BF16, kind="Internal").ap()
    d_xnscr = nc.dram_tensor("xnscr", [tpc, D], BF16, kind="Internal").ap()
    d_bscr = nc.dram_tensor("bscr", [64, 8 * tpc], BF16, kind="Internal").ap()
    d_cscr = nc.dram_tensor("cscr", [64, 8 * tpc], BF16, kind="Internal").ap()
    d_ccin = nc.dram_tensor("ccin", [HSZ + NH], F32, kind="Internal").ap()
    d_dbg = nc.dram_tensor("dbg_yg", [tpc, DI], BF16, kind="Internal").ap() if debug else None
    d_dbg2 = nc.dram_tensor("dbg_yn", [tpc, DI], BF16, kind="Internal").ap() if debug else None
    d_dbg3 = nc.dram_tensor("dbg_h", [64, DI], F32, kind="Internal").ap() if debug else None
    d_dbg4 = nc.dram_tensor("dbg_s", [tpc, D], F32, kind="Internal").ap() if debug else None
    d_dbg5 = nc.dram_tensor("dbg_g", [tpc, D], F32, kind="Internal").ap() if debug else None
    d_ccout = nc.dram_tensor("ccout", [4, HSZ + NH], F32, kind="Internal").ap()

    es = ExitStack()
    SB = lambda nm, shp, dt=F32: es.enter_context(nc.sbuf_tensor(nm, shp, dt))
    PSUMT = lambda nm, shp: es.enter_context(nc.psum_tensor(nm, shp, F32))

    with es:
        ENG = {'v': nc.vector, 's': nc.scalar, 't': nc.tensor, 'g': nc.gpsimd, 'y': nc.sync}
        QSEMS = {'L': ['L0', 'L1', 'L2', 'L3'], 'S': ['S0', 'S1']}
        sem = {}
        for lab in ('v', 's', 't', 'g') + tuple(QSEMS['L']) + tuple(QSEMS['S']):
            sem[lab] = es.enter_context(nc.semaphore("sem_" + lab))
        cnt = {lab: 0 for lab in sem}
        qrr = {'L': 0, 'S': 0}
        waited = {e: {} for e in ENG}
        last_w = {}
        readers = {}
        progs = {e: [] for e in ENG}

        def _wait(ceng, lab, val):
            w = waited[ceng]
            if w.get(lab, -1) >= val:
                return
            s_obj = sem[lab]
            progs[ceng].append(lambda eng, s_obj=s_obj, val=val: eng.wait_ge(s_obj, val))
            w[lab] = val

        def pre(eng, r, w):
            for k in r:
                lw = last_w.get(k)
                if lw and not (eng == 't' and lw[0] == 't'):
                    _wait(eng, lw[0], lw[1])
            for k in w:
                lw = last_w.get(k)
                if lw and not (eng == 't' and lw[0] == 't'):
                    _wait(eng, lw[0], lw[1])
                for lab, v in readers.get(k, {}).items():
                    if not (eng == 't' and lab == 't'):
                        _wait(eng, lab, v)

        def E(eng, op, *args, r=(), w=(), **kw):
            pre(eng, r, w)
            cnt[eng] += 1
            my = cnt[eng]
            s_obj = sem[eng]

            def emit(engobj, op=op, args=args, kw=kw, s_obj=s_obj):
                ins = getattr(engobj, op)(*args, **kw)
                ins.then_inc(s_obj, 1)
            progs[eng].append(emit)
            for k in w:
                last_w[k] = (eng, my)
                readers[k] = {}
            for k in r:
                readers.setdefault(k, {})[eng] = my

        QENG = {'L': 'y', 'S': 's'}

        def DMA(q, out_ap, in_ap, r=(), w=()):
            eng = QENG[q]
            lab = QSEMS[q][qrr[q]]
            qrr[q] = (qrr[q] + 1) % len(QSEMS[q])
            pre(eng, r, w)
            _wait(eng, lab, cnt[lab])
            cnt[lab] += 16
            my = cnt[lab]
            s_obj = sem[lab]

            def emit(engobj, out_ap=out_ap, in_ap=in_ap, s_obj=s_obj):
                engobj.dma_start(out_ap, in_ap).then_inc(s_obj, 16)
            progs[eng].append(emit)
            for k in w:
                last_w[k] = (lab, my)
                readers[k] = {}
            for k in r:
                readers.setdefault(k, {})[lab] = my

        I128f = SB("sI128f", [128, 128]); I128b = SB("sI128b", [128, 128], BF16)
        U128 = SB("sU128", [128, 128])
        ones1 = SB("sones1", [1, 128]); ccons = SB("sccons", [128, 6])
        icons = SB("sicons", [128, 4], mybir.dt.int32)
        nc.const_aps.register(F32, 8388608.0, ccons[:, 4:5])
        nc.const_aps.register(F32, -8388608.0, ccons[:, 5:6])
        wni = SB("wni_s", [128, 8]); wno = SB("wno_s", [128, 16])
        cw_sb = SB("cw_sb", [128, 64]); cb_sb = SB("cb_sb", [128, 16])
        bdt_sb = SB("bdt_sb", [128, 16])
        alog_sb = SB("alog_sb", [8, 1]); negA = SB("negA", [8, 1])
        Dh_row = SB("Dh_row", [1, 8]); Dbc = SB("Dbc", [128, 8])
        fbias_row = SB("fbias_row", [1, D]); biasbc = SB("biasbc", [128, D])
        wsc_11 = SB("wsc_11", [1, 1]); wscbc = SB("wscbc", [128, 1])
        smask_sb = SB("smask_sb", [1, 4])
        Bc_b = [SB(f"Bc_b{i}", [64, 8 * 128], BF16) for i in range(2)]
        Cc_b = [SB(f"Cc_b{i}", [64, 8 * 128], BF16) for i in range(2)]
        explcs = SB("explcs", [128, nch * 8]); expneg = SB("expneg", [128, nch * 8])
        expneg_bf = SB("expneg_bf", [128, nch * 8], BF16)
        tdrow = SB("tdrow", [64, nch * 8])
        h_sb = SB("h_sb", [64, DI]); h_bf = SB("h_bf", [64, DI], BF16)
        Drow = SB("Drow", [1, 8]); el_last = SB("el_last", [1, 8])
        Dg = SB("Dg", [1, 32]); Dp = SB("Dp", [1, 32])
        coef = SB("coef", [1, 8]); cj_row = SB("cj_row", [1, 8]); cjt = SB("cjt", [64, 8])
        hid_tg = [SB(f"hid_tg{i}", [128, D]) for i in range(2)]
        xn_tg = [SB(f"xn_tg{i}", [128, D]) for i in range(2)]
        xn_bf = [SB(f"xn_bf{i}", [128, D], BF16) for i in range(2)]
        stat = [SB(f"stat{i}", [128, 24]) for i in range(4)]
        mv = [SB(f"mv{i}", [128, 2]) for i in range(4)]
        rr_c = [SB(f"rr_c{i}", [128, 1]) for i in range(4)]
        rry_c = SB("rry_c", [128, 1])
        diag = [SB(f"dg_{i}", [128, 128], BF16) for i in range(2)]
        dtA_tm = [SB(f"dtA_tm{i}", [128, 8]) for i in range(2)]

        Wst = [SB(f"Wst{i}", [128, 512]) for i in range(3)]
        Wbf = [SB(f"Wbf{i}", [128, 512], BF16) for i in range(3)]
        x_c = [SB(f"x_c{i}", [128, DI], BF16) for i in range(2)]
        dtA_fm = SB("dtA_fm", [8, tpc])
        xs_col = [SB(f"xs_col{i}", [128, 1]) for i in range(2)]
        xs_all = SB("xs_all", [128, 16])
        xsw = SB("xsw", [128, 1])
        xsr_col = SB("xsr_col", [128, 1])

        groupA = []

        def SBA(nm, shp, dt=F32):
            g = nc.sbuf_tensor(nm, shp, dt)
            t = g.__enter__()
            groupA.append(g)
            return t

        wdt_sb = SBA("wdt_sb", [8, DI])
        oneh = SBA("soneh", [128, 64])
        hid_halo = SBA("hid_halo", [3, D]); xnh = SBA("xnh", [3, D])
        xnh_bf = SBA("xnh_bf", [3, D], BF16)
        stath = SBA("stath", [3, 24]); mvh = SBA("mvh", [3, 2]); rrh_c = SBA("rrh_c", [3, 1])
        diag3 = SBA("diag3", [3, 3], BF16)
        xr_halo = SBA("xr_halo", [128, 24], BF16)
        Wz_bf = SBA("Wz_bf", [128, 8 * 512], BF16)
        xpre = [SBA(f"xpre{i}", [128, 3 + tpc]) for i in range(2)]
        cacc = [SBA(f"cacc{i}", [128, tpc]) for i in range(2)]
        sx_bf = [SBA(f"sx_bf{i}", [128, tpc], BF16) for i in range(2)]
        xev = [SBA(f"xev{i}", [128, 512], BF16) for i in range(2)]
        spbuf = SBA("spbuf", [128, 512])
        szst = [SBA(f"szst{i}", [128, 512], BF16) for i in range(2)]
        dt_raw = SBA("dt_raw", [8, tpc])
        xr_fm = SBA("xr_fm", [128, 8 * tpc], BF16)

        PS = [PSUMT(f"PS{i}", [128, 512]) for i in range(8)]
        PK = [f'PS{i}' for i in range(8)]
        PSB = [p[0:128, 0:64].bitcast(BF16) for p in PS]

        DMA('L', I128f[:], d_I128f, w=('I128f',))
        DMA('L', I128b[:], d_I128b, w=('I128b',))
        DMA('L', U128[:], d_U128, w=('U128',))
        DMA('L', oneh[:], d_oneh, w=('oneh',))
        DMA('L', ones1[:], d_ones, w=('ones1',))
        DMA('L', ccons[:], d_ccons, w=('ccons',))
        DMA('L', icons[:], d_icons, w=('icons',))
        DMA('L', wni[:], d_wni, w=('wni',))
        DMA('L', wno[:], d_wno, w=('wno',))
        DMA('L', cw_sb[:], d_cw, w=('cw',))
        DMA('L', cb_sb[:], d_cb, w=('cb',))
        DMA('L', wdt_sb[:], d_wdt, w=('wdt',))
        DMA('L', bdt_sb[:], d_bdt, w=('bdt',))
        DMA('L', alog_sb[:], d_alog, w=('alog',))
        DMA('L', Dh_row[:], d_Dh, w=('Dh',))
        DMA('L', fbias_row[:], d_fb, w=('fbias',))
        DMA('L', wsc_11[:], d_wsc, w=('wsc',))
        DMA('L', smask_sb[:], d_smask, w=('smask',))
        E('s', 'activation', negA[:], alog_sb[:], AF.Exp, r=('alog',), w=('negA',))
        E('v', 'tensor_scalar', negA[:], negA[:], -1.0 / 256.0, None, ALU.mult,
          r=('negA',), w=('negA',))
        E('t', 'matmul', PS[0][0:128, 0:8], ones1[0:1, :], Dh_row[:],
          start=True, stop=True, r=('ones1', 'Dh'), w=(PK[0],))
        E('v', 'tensor_copy', Dbc[:], PS[0][0:128, 0:8], r=(PK[0],), w=('Dbc',))
        E('t', 'matmul', PS[0][0:128, 0:1], ones1[0:1, :], wsc_11[:],
          start=True, stop=True, r=('ones1', 'wsc'), w=(PK[0],))
        E('v', 'tensor_copy', wscbc[:], PS[0][0:128, 0:1], r=(PK[0],), w=('wscbc',))
        for half in range(2):
            E('t', 'matmul', PS[half][0:128, 0:512], ones1[0:1, :],
              fbias_row[0:1, half * 512:(half + 1) * 512],
              start=True, stop=True, r=('ones1', 'fbias'), w=(PK[half],))
            E('v', 'tensor_copy', biasbc[:, half * 512:(half + 1) * 512],
              PS[half][0:128, 0:512], r=(PK[half],), w=('biasbc',))

        eps_ln = ccons[:, 0:1]
        eps_rms = ccons[:, 1:2]
        one_col = ccons[:, 2:3]
        e127_col = ccons[:, 3:4]

        def ln_rms(hid_buf, hkey, n_p, xn_o, xnkey, st, stkey, mvb, mvkey, rr_col, rrkey):
            nblk = D // 512
            for i in range(nblk):
                E('v', 'bn_stats', st[0:n_p, 6 * i:6 * i + 6],
                  hid_buf[0:n_p, 512 * i:512 * (i + 1)], r=(hkey,), w=(stkey,))
            E('v', 'bn_aggr', mvb[0:n_p, 0:2], st[0:n_p, 0:6 * nblk], r=(stkey,), w=(mvkey,))
            E('v', 'tensor_copy', rr_col[0:n_p, 0:1], mvb[0:n_p, 1:2],
              r=(mvkey,), w=(rrkey,))
            E('s', 'activation', mvb[0:n_p, 1:2], mvb[0:n_p, 1:2], AF.Sqrt,
              bias=eps_ln[0:n_p, 0:1], r=(mvkey, 'ccons'), w=(mvkey,))
            E('v', 'reciprocal', mvb[0:n_p, 1:2], mvb[0:n_p, 1:2], r=(mvkey,), w=(mvkey,))
            E('v', 'tensor_scalar', xn_o[0:n_p, :], hid_buf[0:n_p, :],
              mvb[0:n_p, 0:1], mvb[0:n_p, 1:2], ALU.subtract, ALU.mult,
              r=(hkey, mvkey), w=(xnkey,))
            E('v', 'tensor_scalar', rr_col[0:n_p, 0:1], rr_col[0:n_p, 0:1],
              mvb[0:n_p, 1:2], mvb[0:n_p, 1:2], ALU.mult, ALU.mult,
              r=(mvkey, rrkey), w=(rrkey,))
            E('s', 'activation', rr_col[0:n_p, 0:1], rr_col[0:n_p, 0:1], AF.Sqrt,
              bias=eps_rms[0:n_p, 0:1], r=(rrkey, 'ccons'), w=(rrkey,))
            E('v', 'reciprocal', rr_col[0:n_p, 0:1], rr_col[0:n_p, 0:1], r=(rrkey,), w=(rrkey,))

        DMA('L', hid_halo[:], d_hid[0:3, :], w=('hidh',))
        ln_rms(hid_halo, 'hidh', 3, xnh, 'xnh', stath, 'sth', mvh, 'mvh', rrh_c, 'rrh')
        E('s', 'copy', xnh_bf[:], xnh[:], r=('xnh',), w=('xnhb',))
        E('v', 'tensor_scalar', diag3[:], I128f[0:3, 0:3], rrh_c[0:3, 0:1], None,
          ALU.mult, r=('I128f', 'rrh'), w=('diag3',))
        for kt in range(8):
            E('t', 'matmul', PS[0][0:128, 0:3], xnh_bf[0:3, kt * 128:(kt + 1) * 128],
              diag3[:], start=True, stop=True, r=('xnhb', 'diag3'), w=(PK[0],))
            E('v', 'tensor_copy', xr_halo[:, kt * 3:kt * 3 + 3], PS[0][0:128, 0:3],
              r=(PK[0],), w=('xrh',))
        for tg in range(ntg):
            p = tg % 4
            hb = hid_tg[p % 2]
            hkey = f'hid{p % 2}'
            DMA('L', hb[:], d_hid[3 + tg * 128: 3 + (tg + 1) * 128, :], w=(hkey,))
            ln_rms(hb, hkey, 128, xn_tg[p % 2], f'xn{p % 2}', stat[p], f'st{p}', mv[p],
                   f'mv{p}', rr_c[p], f'rr{p}')
            E('s', 'copy', xn_bf[p % 2][:], xn_tg[p % 2][:], r=(f'xn{p % 2}',),
              w=(f'xnb{p % 2}',))
            E('v', 'tensor_scalar', diag[p % 2][:], I128f[:], rr_c[p][:, 0:1], None, ALU.mult,
              r=('I128f', f'rr{p}'), w=(f'diag{p % 2}',))
            for kt in range(8):
                bk = kt // 4
                E('t', 'matmul', PS[bk][0:128, (kt % 4) * 128:(kt % 4) * 128 + 128],
                  xn_bf[p % 2][:, kt * 128:(kt + 1) * 128], diag[p % 2][:],
                  start=True, stop=True, r=(f'xnb{p % 2}', f'diag{p % 2}'), w=(PK[bk],))
            xr_r = xr_fm[:, :].rearrange("q (k t) -> q k t", k=8)
            for bk in range(2):
                E('s', 'copy',
                  xr_r[:, bk * 4:(bk + 1) * 4, tg * 128:(tg + 1) * 128],
                  PS[bk][0:128, 0:512].rearrange("q (k t) -> q k t", k=4),
                  r=(PK[bk],), w=('xr',))
            DMA('S', d_xnscr[tg * 128:(tg + 1) * 128, :], xn_bf[p % 2][:],
                r=(f'xnb{p % 2}',), w=('xnscr',))
            wzf = Wz_bf[:, :].bitcast(F32)
            slA = wzf[:, 0:1024]
            slB = wzf[:, 1024:2048]
            slAi = Wz_bf[:, :].bitcast(mybir.dt.int32)[:, 0:1024]
            slBi = Wz_bf[:, :].bitcast(mybir.dt.int32)[:, 1024:2048]
            E('v', 'reduce_max', xs_all[:, tg:tg + 1], xn_tg[p % 2][:], axis=AX,
              apply_absolute_value=True, r=(f'xn{p % 2}',), w=('xsall',))
            E('v', 'tensor_scalar', xs_all[:, tg:tg + 1], xs_all[:, tg:tg + 1],
              1e-12, 1.0 / 448.0, ALU.max, ALU.mult, r=('xsall',), w=('xsall',))
            E('v', 'reciprocal', xsr_col[:], xs_all[:, tg:tg + 1], r=('xsall',), w=('xsr',))
            E('s', 'activation', slA, xn_tg[p % 2][:], AF.Abs, scale=xsr_col[:],
              r=(f'xn{p % 2}', 'xsr'), w=('wz',))
            E('v', 'tensor_scalar', slBi, slAi, 23, None, ALU.arith_shift_right,
              r=('wz',), w=('wz',))
            E('v', 'tensor_scalar', slBi, slBi, 121, None, ALU.max, r=('wz',), w=('wz',))
            E('v', 'tensor_scalar', slBi, slBi, -1, 257, ALU.mult, ALU.add,
              r=('wz',), w=('wz',))
            E('v', 'tensor_scalar', slBi, slBi, 23, None, ALU.arith_shift_left,
              r=('wz',), w=('wz',))
            E('v', 'tensor_tensor', slA, slA, slB, ALU.mult, r=('wz',), w=('wz',))
            E('s', 'add', slA, slA, 8388608.0, r=('wz', 'ccons'), w=('wz',))
            E('s', 'add', slA, slA, -8388608.0, r=('wz', 'ccons'), w=('wz',))
            E('v', 'tensor_scalar', slBi, slBi, 23, None, ALU.arith_shift_right,
              r=('wz',), w=('wz',))
            E('v', 'tensor_scalar', slBi, slBi, -1, 254, ALU.mult, ALU.add,
              r=('wz',), w=('wz',))
            E('v', 'tensor_scalar', slBi, slBi, 23, None, ALU.arith_shift_left,
              r=('wz',), w=('wz',))
            E('v', 'tensor_tensor', slA, slA, slB, ALU.mult, r=('wz',), w=('wz',))
            E('s', 'activation', slB, xn_bf[p % 2][:], AF.Sign,
              r=(f'xnb{p % 2}', 'xnscr'), w=('wz',))
            E('v', 'tensor_tensor', xn_bf[p % 2][:], slA, slB, ALU.mult,
              r=('wz',), w=(f'xnb{p % 2}',))
            for kt in range(8):
                bk = kt // 4
                pbq = PS[bk][0:128, 0:512].bitcast(BF16)
                E('t', 'transpose', pbq[:, (kt % 4) * 128:(kt % 4) * 128 + 128],
                  xn_bf[p % 2][:, kt * 128:(kt + 1) * 128], I128b[:],
                  r=(f'xnb{p % 2}', 'I128b'), w=(PK[bk],))
            for bk in range(2):
                E('v', 'tensor_copy', xev[bk][:],
                  PS[bk][0:128, 0:512].bitcast(BF16)[:, 0:512],
                  r=(PK[bk],), w=(f'xev{bk}',))
                DMA('S',
                    d_qscr[bk * 512:(bk + 1) * 512, tg * 128:(tg + 1) * 128]
                    .rearrange("(a q) c -> q a c", q=128),
                    xev[bk][:, :].rearrange("q (a c) -> q a c", a=4),
                    r=(f'xev{bk}',), w=('qscr',))

        def load_w_chunk(bi, kt, c0, width):
            DMA('L', Wst[bi][:, 0:width],
                d_win[kt * 128:(kt + 1) * 128, c0:c0 + width], w=(f'wst{bi}',))
            E('s', 'activation', Wbf[bi][:, 0:width], Wst[bi][:, 0:width], AF.Copy,
              scale=wni[:, kt:kt + 1], r=(f'wst{bi}', 'wni'), w=(f'wbf{bi}',))

        for ct in range(16):
            for kt in range(8):
                bi = kt % 3
                load_w_chunk(bi, kt, 2048 + ct * 128, 128)
                E('t', 'matmul', PS[2][0:128, 0:3], Wbf[bi][:, 0:128],
                  xr_halo[:, kt * 3:kt * 3 + 3], start=(kt == 0), stop=(kt == 7),
                  r=(f'wbf{bi}', 'xrh'), w=(PK[2],))
                for q in range(nq):
                    E('t', 'matmul', PS[3 + q][0:128, 0:512], Wbf[bi][:, 0:128],
                      xr_fm[:, kt * tpc + q * 512: kt * tpc + (q + 1) * 512],
                      start=(kt == 0), stop=(kt == 7),
                      r=(f'wbf{bi}', 'xr'), w=(PK[3 + q],))
            p = ct % 2
            E('v', 'tensor_copy', xpre[p][:, 0:3], PS[2][0:128, 0:3], r=(PK[2],), w=(f'xpre{p}',))
            for q in range(nq):
                E('v', 'tensor_copy', xpre[p][:, 3 + q * 512: 3 + (q + 1) * 512],
                  PS[3 + q][0:128, 0:512], r=(PK[3 + q],), w=(f'xpre{p}',))
            E('v', 'tensor_scalar', cacc[p][:, 0:tpc], xpre[p][:, 0:tpc],
              cw_sb[:, ct * 4:ct * 4 + 1], None, ALU.mult, r=(f'xpre{p}', 'cw'), w=(f'cacc{p}',))
            for j in (1, 2, 3):
                E('v', 'scalar_tensor_tensor', cacc[p][:, 0:tpc], xpre[p][:, j:j + tpc],
                  cw_sb[:, ct * 4 + j:ct * 4 + j + 1], cacc[p][:, 0:tpc],
                  ALU.mult, ALU.add, r=(f'xpre{p}', 'cw', f'cacc{p}'), w=(f'cacc{p}',))
            E('s', 'activation', sx_bf[p][:, 0:tpc], cacc[p][:, 0:tpc], AF.Silu,
              bias=cb_sb[:, ct:ct + 1], r=(f'cacc{p}', 'cb'), w=(f'sx{p}',))
            for g in range(nch):
                bi = (g // 4) % 2
                pb = PS[bi][0:128, 0:512].bitcast(BF16)
                E('t', 'transpose', pb[:, (g % 4) * 128:(g % 4) * 128 + 128],
                  sx_bf[p][:, g * 128:(g + 1) * 128], I128b[:],
                  r=(f'sx{p}', 'I128b'), w=(PK[bi],))
                if g % 4 == 3:
                    E('v', 'tensor_copy', xev[bi][:], pb[:, 0:512], r=(PK[bi],),
                      w=(f'xev{bi}',))
                    DMA('S',
                        d_xscr[(g - 3) * 128:(g + 1) * 128, ct * 128:(ct + 1) * 128]
                        .rearrange("(a q) c -> q a c", q=128),
                        xev[bi][:, :].rearrange("q (a c) -> q a c", a=4),
                        r=(f'xev{bi}',), w=('xscr',))

        for dscr, dkey, base in ((d_bscr, 'bscr', 4096), (d_cscr, 'cscr', 4608)):
            for ct in range(4):
                for kt in range(8):
                    bi = kt % 3
                    load_w_chunk(bi, kt, base + ct * 128, 128)
                    for q in range(nq):
                        E('t', 'matmul', PS[3 + q][0:128, 0:512], Wbf[bi][:, 0:128],
                          xr_fm[:, kt * tpc + q * 512: kt * tpc + (q + 1) * 512],
                          start=(kt == 0), stop=(kt == 7),
                          r=(f'wbf{bi}', 'xr'), w=(PK[3 + q],))
                for q in range(nq):
                    sti = q % 2
                    E('s', 'copy', szst[sti][:], PS[3 + q][0:128, 0:512],
                      r=(PK[3 + q],), w=(f'szst{sti}',))
                    DMA('S', dscr[:, (2 * ct) * tpc + q * 512: (2 * ct) * tpc + (q + 1) * 512],
                        szst[sti][0:64, :], r=(f'szst{sti}',), w=(dkey,))
                    DMA('S', dscr[:, (2 * ct + 1) * tpc + q * 512: (2 * ct + 1) * tpc + (q + 1) * 512],
                        szst[sti][64:128, :], r=(f'szst{sti}',), w=(dkey,))

        for kt in range(8):
            bi = kt % 3
            load_w_chunk(bi, kt, 5120, 8)
            for q in range(nq):
                E('t', 'matmul', PS[3 + q][0:8, 0:512], Wbf[bi][:, 0:8],
                  xr_fm[:, kt * tpc + q * 512: kt * tpc + (q + 1) * 512],
                  start=(kt == 0), stop=(kt == 7),
                  r=(f'wbf{bi}', 'xr'), w=(PK[3 + q],))
        for q in range(nq):
            E('v', 'tensor_copy', dt_raw[:, q * 512:(q + 1) * 512],
              PS[3 + q][0:8, 0:512], r=(PK[3 + q],), w=('dt_raw',))
        for ct in range(16):
            for q in range(nq):
                E('t', 'matmul', PS[0][0:128, 0:512], wdt_sb[:, ct * 128:(ct + 1) * 128],
                  dt_raw[:, q * 512:(q + 1) * 512], start=True, stop=True,
                  r=('wdt', 'dt_raw'), w=(PK[0],))
                E('s', 'activation', spbuf[:, 0:512], PS[0][0:128, 0:512], AF.Exp,
                  bias=bdt_sb[:, ct:ct + 1], r=(PK[0], 'bdt'), w=('spbuf',))
                E('s', 'activation', spbuf[:, 0:512], spbuf[:, 0:512], AF.Ln,
                  bias=one_col[:, 0:1], r=('spbuf', 'ccons'), w=('spbuf',))
                E('t', 'matmul', PS[4 + q][0:8, 0:512],
                  oneh[:, (ct // 2) * 8:(ct // 2) * 8 + 8], spbuf[:, 0:512],
                  start=(ct == 0), stop=(ct == 15),
                  r=('oneh', 'spbuf'), w=(PK[4 + q],))
        for q in range(nq):
            E('v', 'tensor_scalar', dtA_fm[:, q * 512:(q + 1) * 512], PS[4 + q][0:8, 0:512],
              negA[0:8, 0:1], None, ALU.mult, r=(PK[4 + q], 'negA'), w=('dtA',))

        for zc in range(4):
            for kt in range(8):
                bi = kt % 3
                DMA('L', Wst[bi][:, 0:512],
                    d_win[kt * 128:(kt + 1) * 128, zc * 512:(zc + 1) * 512], w=(f'wst{bi}',))
                E('s', 'activation', Wz_bf[:, kt * 512:(kt + 1) * 512], Wst[bi][:, 0:512],
                  AF.Copy, scale=wni[:, kt:kt + 1], r=(f'wst{bi}', 'wni'), w=('wz',))
            for tg in range(ntg):
                for kt in range(8):
                    E('t', 'matmul', PS[tg % 2][0:128, 0:512],
                      xr_fm[:, kt * tpc + tg * 128:kt * tpc + (tg + 1) * 128],
                      Wz_bf[:, kt * 512:(kt + 1) * 512],
                      start=(kt == 0), stop=(kt == 7),
                      r=('xr', 'wz'), w=(PK[tg % 2],))
                bi = tg % 2
                E('s', 'activation', szst[bi][:], PS[tg % 2][0:128, 0:512], AF.Silu,
                  r=(PK[tg % 2],), w=(f'szst{bi}',))
                DMA('S', d_zscr[tg * 128:(tg + 1) * 128, zc * 512:(zc + 1) * 512],
                    szst[bi][:], r=(f'szst{bi}',), w=('zscr',))

        freed_keys = ('xr', 'xrh', 'wdt', 'oneh', 'hidh', 'xnh', 'xnhb', 'sth',
                      'mvh', 'rrh', 'diag3', 'wz', 'xpre0', 'xpre1', 'cacc0', 'cacc1',
                      'sx0', 'sx1', 'xev0', 'xev1', 'spbuf', 'szst0', 'szst1', 'dt_raw')
        pre('y', freed_keys, freed_keys)
        for g_ in reversed(groupA):
            g_.__exit__(None, None, None)
        groupB = []

        def SBB(nm, shp, dt=F32):
            g = nc.sbuf_tensor(nm, shp, dt)
            t = g.__enter__()
            groupB.append(g)
            return t

        Hg = SBB("Hg", [64, DI])
        Wout_bf = SBB("Wout_bf", [128, 16 * 1024], BF16)
        WffnT = SBB("WffnT", [128, 8 * 1024], BF16)
        wu8 = SBB("wu8", [128, 1024], mybir.dt.uint8)
        wf8b = SBB("wf8b", [128, 1024], BF16)
        db1 = SBB("db1", [128, 1024]); db2 = SBB("db2", [128, 1024])
        db3 = SBB("db3", [128, 1024]); db4 = SBB("db4", [128, 1024])
        dbi = SBB("dbi", [128, 1024], mybir.dt.int32)
        sz_c = [SBB(f"sz_c{i}", [128, DI], BF16) for i in range(2)]
        yg_sb = [SBB(f"yg_sb{i}", [128, DI], BF16) for i in range(2)]
        yn_fm = [SBB(f"yn_fm{i}", [128, DI], BF16) for i in range(2)]
        out_buf = [SBB(f"out_buf{i}", [128, D]) for i in range(2)]
        xnl = [SBB(f"xnl{i}", [128, D], BF16) for i in range(2)]
        B_tm4 = [SBB(f"B_tm4_{i}", [128, 64], BF16) for i in range(4)]
        Sbf4 = [SBB(f"Sbf4_{i}", [128, 128], BF16) for i in range(4)]
        wx4 = [SBB(f"wx4_{i}", [128, HD], BF16) for i in range(4)]
        yb4 = [SBB(f"yb4_{i}", [128, HD]) for i in range(4)]
        q_fm = [SBB(f"q_fm{i}", [128, D], BF16) for i in range(2)]

        for ct in range(16):
            for half in range(2):
                bi = half
                DMA('L', Wst[bi][:, 0:512],
                    d_wout[ct * 128:(ct + 1) * 128, half * 512:(half + 1) * 512],
                    w=(f'wst{bi}',))
                E('s', 'activation',
                  Wout_bf[:, ct * 1024 + half * 512: ct * 1024 + (half + 1) * 512],
                  Wst[bi][:, 0:512], AF.Copy, scale=wno[:, ct:ct + 1],
                  r=(f'wst{bi}', 'wno'), w=('wout',))
        for ot in range(8):
            DMA('L', wu8[:], d_wffn[ot * 128:(ot + 1) * 128, :], w=('wf8',))
            E('s', 'copy', db1[:], wu8[:], r=('wf8',), w=('db1',))
            E('v', 'tensor_scalar', db2[:], db1[:], 127.5, None, ALU.is_ge,
              r=('db1',), w=('db2',))
            E('v', 'scalar_tensor_tensor', db3[:], db2[:], -128.0, db1[:],
              ALU.mult, ALU.add, r=('db1', 'db2'), w=('db3',))
            E('v', 'tensor_scalar', db2[:], db2[:], -2.0, 1.0, ALU.mult, ALU.add,
              r=('db2',), w=('db2',))
            E('v', 'tensor_scalar', db4[:], db3[:], 0.125, -0.4375, ALU.mult, ALU.add,
              r=('db3',), w=('db4',))
            E('s', 'add', db4[:], db4[:], 8388608.0, r=('db4', 'ccons'), w=('db4',))
            E('s', 'add', db4[:], db4[:], -8388608.0, r=('db4', 'ccons'), w=('db4',))
            E('v', 'scalar_tensor_tensor', db4[:], db4[:], -8.0, db3[:],
              ALU.mult, ALU.add, r=('db3', 'db4'), w=('db4',))
            E('v', 'scalar_tensor_tensor', db3[:], db4[:], -1.0, db3[:],
              ALU.mult, ALU.add, r=('db3', 'db4'), w=('db3',))
            E('v', 'tensor_scalar', db3[:], db3[:], 0.125, None, ALU.mult,
              r=('db3',), w=('db3',))
            E('v', 'tensor_scalar', db1[:], db3[:], 8.0, 960.0,
              ALU.mult, ALU.add, r=('db3',), w=('db1',))
            E('v', 'tensor_tensor', db1[:], db1[:], db4[:], ALU.add,
              r=('db1', 'db4'), w=('db1',))
            E('s', 'copy', dbi[:], db1[:], r=('db1',), w=('dbi',))
            E('v', 'tensor_scalar', dbi[:], dbi[:], 20, None, ALU.arith_shift_left,
              r=('dbi',), w=('dbi',))
            E('v', 'tensor_scalar', db3[:], db3[:], 0.5, None, ALU.is_ge,
              r=('db3',), w=('db3',))
            E('v', 'tensor_scalar', db4[:], db4[:], 0.001953125, None, ALU.mult,
              r=('db4',), w=('db4',))
            E('v', 'tensor_tensor', db1[:], dbi[:, :].bitcast(F32),
              db4[:], ALU.subtract, r=('dbi', 'db4'), w=('db1',))
            E('v', 'tensor_tensor', db1[:], db1[:], db3[:], ALU.mult,
              r=('db1', 'db3'), w=('db1',))
            E('v', 'tensor_tensor', db1[:], db1[:], db4[:],
              ALU.add, r=('db1', 'db4'), w=('db1',))
            E('v', 'tensor_tensor', wf8b[:], db1[:], db2[:], ALU.mult,
              r=('db1', 'db2'), w=('wf8b',))
            for kt in range(8):
                E('t', 'transpose', PSB[kt % 2], wf8b[:, kt * 128:(kt + 1) * 128], I128b[:],
                  r=('wf8b', 'I128b'), w=(PK[kt % 2],))
                E('s', 'copy', WffnT[:, kt * 1024 + ot * 128: kt * 1024 + (ot + 1) * 128],
                  PSB[kt % 2], r=(PK[kt % 2],), w=('wffnT',))

        bscr_r = d_bscr.rearrange("p (h t) -> p h t", h=8)
        cscr_r = d_cscr.rearrange("p (h t) -> p h t", h=8)
        Bc_r = [b_[:, :].rearrange("p (h t) -> p h t", h=8) for b_ in Bc_b]
        Cc_r = [c_[:, :].rearrange("p (h t) -> p h t", h=8) for c_ in Cc_b]
        HKEYS = tuple(f'hsb{h}' for h in range(NH))
        E('v', 'memset', h_sb[:], 0.0, w=HKEYS)
        DMA('L', x_c[0][:], d_xscr[0:128, :], r=('xscr',), w=('xc0',))
        DMA('L', Bc_r[0], bscr_r[:, :, 0:128], r=('bscr',), w=('Bc0',))
        for c in range(nch):
            xcb = x_c[c % 2]
            xckey = f'xc{c % 2}'
            if c + 1 < nch:
                DMA('L', x_c[(c + 1) % 2][:], d_xscr[(c + 1) * 128:(c + 2) * 128, :],
                    r=('xscr',), w=(f'xc{(c + 1) % 2}',))
                DMA('L', Bc_r[(c + 1) % 2], bscr_r[:, :, (c + 1) * 128:(c + 2) * 128],
                    r=('bscr',), w=(f'Bc{(c + 1) % 2}',))
            E('t', 'transpose', PS[0][0:128, (c % 2) * 8:(c % 2) * 8 + 8],
              dtA_fm[:, c * 128:(c + 1) * 128],
              I128f[0:8, 0:8], r=('dtA', 'I128f'), w=(PK[0],))
            E('v', 'tensor_copy', dtA_tm[c % 2][:], PS[0][0:128, (c % 2) * 8:(c % 2) * 8 + 8],
              r=(PK[0],), w=(f'dtA_tm{c % 2}',))
            E('t', 'matmul', PS[1][0:128, (c % 2) * 8:(c % 2) * 8 + 8], U128[:], dtA_tm[c % 2][:],
              start=True, stop=True, r=('U128', f'dtA_tm{c % 2}'), w=(PK[1],))
            E('s', 'activation', explcs[:, c * 8:(c + 1) * 8],
              PS[1][0:128, (c % 2) * 8:(c % 2) * 8 + 8], AF.Exp,
              r=(PK[1],), w=('explcs',))
            E('v', 'tensor_scalar', expneg[:, c * 8:(c + 1) * 8],
              PS[1][0:128, (c % 2) * 8:(c % 2) * 8 + 8],
              -1.0, None, ALU.mult, r=(PK[1],), w=('expneg',))
            E('s', 'activation', expneg[:, c * 8:(c + 1) * 8], expneg[:, c * 8:(c + 1) * 8],
              AF.Exp, r=('expneg',), w=('expneg',))
            E('s', 'copy', expneg_bf[:, c * 8:(c + 1) * 8], expneg[:, c * 8:(c + 1) * 8],
              r=('expneg',), w=('expneg_bf',))
            E('t', 'matmul', PS[2][0:1, 0:8], e127_col, explcs[:, c * 8:(c + 1) * 8],
              start=True, stop=True, r=('ccons', 'explcs'), w=(PK[2],))
            E('v', 'tensor_copy', el_last[:], PS[2][0:1, 0:8], r=(PK[2],), w=('el_last',))
            E('t', 'matmul', PS[2][0:64, 0:8], ones1[0:1, 0:64], el_last[:],
              start=True, stop=True, r=('ones1', 'el_last'), w=(PK[2],))
            E('v', 'tensor_copy', tdrow[:, c * 8:(c + 1) * 8], PS[2][0:64, 0:8],
              r=(PK[2],), w=('tdrow',))
            PSB3 = PS[3][0:128, 0:512].bitcast(BF16)
            for h in range(NH):
                E('t', 'transpose', PSB3[:, h * 64:h * 64 + 64],
                  Bc_b[c % 2][:, h * 128:(h + 1) * 128], I128b[0:64, 0:64],
                  r=(f'Bc{c % 2}', 'I128b'), w=(PK[3],))
            for g in range(2):
                hs = [4 * g + i for i in range(4)]
                for hh, h in enumerate(hs):
                    E('s', 'copy', B_tm4[hh][:], PSB3[:, h * 64:h * 64 + 64],
                      r=(PK[3],), w=(f'Bt{hh}',))
                for hh, h in enumerate(hs):
                    E('v', 'tensor_scalar', wx4[hh][:], xcb[:, h * HD:(h + 1) * HD],
                      expneg[:, c * 8 + h:c * 8 + h + 1], None, ALU.mult,
                      r=(xckey, 'expneg'), w=(f'wx{hh}',))
                for hh, h in enumerate(hs):
                    ps = PS[4 + 2 * g + hh // 2][0:64, (hh % 2) * HD:(hh % 2) * HD + HD]
                    E('t', 'matmul', ps, B_tm4[hh][:, 0:64], wx4[hh][:],
                      start=True, stop=True, r=(f'Bt{hh}', f'wx{hh}'),
                      w=(PK[4 + 2 * g + hh // 2],))
                for hh, h in enumerate(hs):
                    ps = PS[4 + 2 * g + hh // 2][0:64, (hh % 2) * HD:(hh % 2) * HD + HD]
                    hhs = h_sb[:, h * HD:(h + 1) * HD]
                    E('v', 'tensor_tensor', hhs, hhs, ps, ALU.add,
                      r=(f'hsb{h}', PK[4 + 2 * g + hh // 2]), w=(f'hsb{h}',))
                    E('v', 'tensor_scalar', hhs, hhs, tdrow[:, c * 8 + h:c * 8 + h + 1],
                      None, ALU.mult, r=(f'hsb{h}', 'tdrow'), w=(f'hsb{h}',))
        E('v', 'memset', Drow[:], 1.0, w=('Drow',))
        for c in range(nch):
            E('v', 'tensor_tensor', Drow[:], Drow[:], tdrow[0:1, c * 8:(c + 1) * 8],
              ALU.mult, r=('Drow', 'tdrow'), w=('Drow',))

        DMA('L', d_ccin[0:HSZ].rearrange("(p f) -> p f", p=64), h_sb[:],
            r=HKEYS, w=('ccin',))
        DMA('L', d_ccin[HSZ:HSZ + 8].rearrange("(a b) -> a b", a=1), Drow[:],
            r=('Drow',), w=('ccin',))
        pre('g', ('ccin',), ('ccout',))
        cnt['g'] += 1
        ccmy = cnt['g']

        def emit_cc(engobj):
            ins = engobj.collective_compute(
                "AllGather", ALU.bypass,
                replica_groups=[[0, 1, 2, 3], [4, 5, 6, 7]],
                ins=[d_ccin], outs=[d_ccout])
            ins.then_inc(sem['g'], 1)
        progs['g'].append(emit_cc)
        last_w['ccout'] = ('g', ccmy)
        readers['ccout'] = {}
        for j in range(4):
            DMA('L', Dg[:, j * 8:(j + 1) * 8], d_ccout[j:j + 1, HSZ:HSZ + 8],
                r=('ccout',), w=('Dga',))
        for j in range(4):
            E('v', 'tensor_scalar', Dp[:, j * 8:(j + 1) * 8], Dg[:, j * 8:(j + 1) * 8],
              1.0, smask_sb[0:1, j:j + 1], ALU.subtract, ALU.mult,
              r=('Dga', 'smask'), w=('Dpa',))
            E('v', 'tensor_scalar', Dp[:, j * 8:(j + 1) * 8], Dp[:, j * 8:(j + 1) * 8],
              1.0, None, ALU.add, r=('Dpa',), w=('Dpa',))
        E('v', 'memset', h_sb[:], 0.0, w=HKEYS)
        E('v', 'memset', coef[:], 1.0, w=('coef',))
        for j in (3, 2, 1, 0):
            DMA('L', Hg[:], d_ccout[j:j + 1, 0:HSZ].rearrange("a (p f) -> (a p) f", p=64),
                r=('ccout',), w=('Hga',))
            E('v', 'tensor_scalar', cj_row[:], coef[:], smask_sb[0:1, j:j + 1], None,
              ALU.mult, r=('coef', 'smask'), w=('cj',))
            E('t', 'matmul', PS[0][0:64, 0:8], ones1[0:1, 0:64], cj_row[:],
              start=True, stop=True, r=('ones1', 'cj'), w=(PK[0],))
            E('v', 'tensor_copy', cjt[:], PS[0][0:64, 0:8], r=(PK[0],), w=('cjt',))
            for h in range(NH):
                E('v', 'scalar_tensor_tensor', h_sb[:, h * HD:(h + 1) * HD],
                  Hg[:, h * HD:(h + 1) * HD], cjt[:, h:h + 1],
                  h_sb[:, h * HD:(h + 1) * HD], ALU.mult, ALU.add,
                  r=('Hga', 'cjt', f'hsb{h}'), w=(f'hsb{h}',))
            E('v', 'tensor_tensor', coef[:], coef[:], Dp[:, j * 8:(j + 1) * 8], ALU.mult,
              r=('coef', 'Dpa'), w=('coef',))

        if debug:
            DMA('S', d_dbg3[0:64, :], h_sb[:], r=HKEYS)

        DMA('L', sz_c[0][:], d_zscr[0:128, :], r=('zscr',), w=('sz0',))
        DMA('L', x_c[0][:], d_xscr[0:128, :], r=('xscr',), w=('xc0',))
        DMA('L', Bc_r[0], bscr_r[:, :, 0:128], r=('bscr',), w=('Bc0',))
        DMA('L', Cc_r[0], cscr_r[:, :, 0:128], r=('cscr',), w=('Cc0',))

        def scan_emit(c):
            cp = c % 2
            szb = sz_c[cp]
            szkey = f'sz{cp}'
            xcb = x_c[cp]
            xckey = f'xc{cp}'
            bkey = f'Bc{cp}'
            ckey = f'Cc{cp}'
            if c + 1 < nch:
                np1 = (c + 1) % 2
                DMA('L', sz_c[np1][:], d_zscr[(c + 1) * 128:(c + 2) * 128, :],
                    r=('zscr',), w=(f'sz{np1}',))
                DMA('L', x_c[np1][:], d_xscr[(c + 1) * 128:(c + 2) * 128, :],
                    r=('xscr',), w=(f'xc{np1}',))
                DMA('L', Bc_r[np1], bscr_r[:, :, (c + 1) * 128:(c + 2) * 128],
                    r=('bscr',), w=(f'Bc{np1}',))
                DMA('L', Cc_r[np1], cscr_r[:, :, (c + 1) * 128:(c + 2) * 128],
                    r=('cscr',), w=(f'Cc{np1}',))
            E('s', 'copy', h_bf[:], h_sb[:], r=HKEYS, w=('h_bf',))
            for g in range(2):
                hs = [4 * g + i for i in range(4)]
                for hh, h in enumerate(hs):
                    bsl = Bc_b[cp][:, h * 128:(h + 1) * 128]
                    csl = Cc_b[cp][:, h * 128:(h + 1) * 128]
                    E('t', 'matmul', PS[0][0:128, hh * 128:(hh + 1) * 128], bsl, csl,
                      start=True, stop=True, r=(bkey, ckey), w=(PK[0],))
                for hh, h in enumerate(hs):
                    E('s', 'activation', wx4[hh][:],
                      xcb[:, h * HD:(h + 1) * HD], AF.Copy,
                      scale=expneg[:, c * 8 + h:c * 8 + h + 1],
                      r=(xckey, 'expneg'), w=(f'wx{hh}',))
                for hh, h in enumerate(hs):
                    E('v', 'tensor_tensor', Sbf4[hh][:],
                      PS[0][0:128, hh * 128:(hh + 1) * 128], U128[:], ALU.mult,
                      r=(PK[0], 'U128'), w=(f'Sb{hh}',))
                for hh, h in enumerate(hs):
                    csl = Cc_b[cp][:, h * 128:(h + 1) * 128]
                    yb = PS[1 + hh // 2][0:128, (hh % 2) * HD:(hh % 2) * HD + HD]
                    E('t', 'matmul', yb, Sbf4[hh][:], wx4[hh][:], start=True, stop=False,
                      r=(f'Sb{hh}', f'wx{hh}'), w=(PK[1 + hh // 2],))
                    E('t', 'matmul', yb, csl, h_bf[:, h * HD:(h + 1) * HD],
                      start=False, stop=True, r=(ckey, 'h_bf'), w=(PK[1 + hh // 2],))
                PSB0 = PS[0][0:128, 0:512].bitcast(BF16)
                for hh, h in enumerate(hs):
                    bsl = Bc_b[cp][:, h * 128:(h + 1) * 128]
                    E('t', 'transpose', PSB0[:, hh * 128:hh * 128 + 64], bsl,
                      I128b[0:64, 0:64], r=(bkey, 'I128b'), w=(PK[0],))
                for hh, h in enumerate(hs):
                    E('s', 'copy', B_tm4[hh][:], PSB0[:, hh * 128:hh * 128 + 64],
                      r=(PK[0],), w=(f'Bt{hh}',))
                for hh, h in enumerate(hs):
                    ps = PS[3 + hh // 2][0:64, (hh % 2) * HD:(hh % 2) * HD + HD]
                    E('t', 'matmul', ps, B_tm4[hh][:, 0:64], wx4[hh][:],
                      start=True, stop=True, r=(f'Bt{hh}', f'wx{hh}'), w=(PK[3 + hh // 2],))
                for hh, h in enumerate(hs):
                    yb = PS[1 + hh // 2][0:128, (hh % 2) * HD:(hh % 2) * HD + HD]
                    elcol = explcs[:, c * 8 + h:c * 8 + h + 1]
                    E('v', 'tensor_scalar', yb4[hh][:], yb, elcol, None, ALU.mult,
                      r=(PK[1 + hh // 2], 'explcs'), w=(f'yb{hh}',))
                    E('v', 'scalar_tensor_tensor', yb4[hh][:],
                      xcb[:, h * HD:(h + 1) * HD], Dbc[:, h:h + 1], yb4[hh][:],
                      ALU.mult, ALU.add, r=(xckey, 'Dbc', f'yb{hh}'), w=(f'yb{hh}',))
                    E('v', 'tensor_tensor', yg_sb[cp][:, h * HD:(h + 1) * HD], yb4[hh][:],
                      szb[:, h * HD:(h + 1) * HD], ALU.mult, r=(f'yb{hh}', szkey),
                      w=(f'yg{cp}_{h}',))
                for hh, h in enumerate(hs):
                    ps = PS[3 + hh // 2][0:64, (hh % 2) * HD:(hh % 2) * HD + HD]
                    hhs = h_sb[:, h * HD:(h + 1) * HD]
                    E('v', 'tensor_tensor', hhs, hhs, ps, ALU.add,
                      r=(f'hsb{h}', PK[3 + hh // 2]), w=(f'hsb{h}',))
                    E('v', 'tensor_scalar', hhs, hhs, tdrow[:, c * 8 + h:c * 8 + h + 1],
                      None, ALU.mult, r=(f'hsb{h}', 'tdrow'), w=(f'hsb{h}',))

        def epi_emit(c):
            cp = c % 2
            YGK = tuple(f'yg{cp}_{h}' for h in range(NH))
            if debug:
                DMA('S', d_dbg[c * 128:(c + 1) * 128, :], yg_sb[cp][:], r=YGK)
            for i in range(4):
                E('v', 'bn_stats', stat[cp][:, 6 * i:6 * i + 6],
                  yg_sb[cp][:, 512 * i:512 * (i + 1)], r=YGK, w=(f'st{cp}',))
            E('v', 'bn_aggr', mv[cp][:, 0:2], stat[cp][:, 0:24], r=(f'st{cp}',), w=(f'mv{cp}',))
            E('v', 'tensor_scalar', rry_c[:], mv[cp][:, 0:1], mv[cp][:, 0:1], mv[cp][:, 1:2],
              ALU.mult, ALU.add, r=(f'mv{cp}',), w=('rry',))
            E('s', 'activation', rry_c[:], rry_c[:], AF.Sqrt,
              bias=eps_rms[:, 0:1], r=('rry', 'ccons'), w=('rry',))
            E('v', 'reciprocal', rry_c[:], rry_c[:], r=('rry',), w=('rry',))
            E('v', 'tensor_scalar', diag[cp][:], I128f[:], rry_c[:, 0:1], None, ALU.mult,
              r=('I128f', 'rry'), w=(f'diag{cp}',))
            yn_r = yn_fm[cp][:, :].rearrange("q (a t) -> q a t", a=16)
            for ct in range(16):
                E('t', 'matmul', PS[5][0:128, (ct % 4) * 128:(ct % 4) * 128 + 128],
                  yg_sb[cp][:, ct * 128:(ct + 1) * 128], diag[cp][:], start=True, stop=True,
                  r=YGK + (f'diag{cp}',), w=(PK[5],))
                if ct % 4 == 3:
                    E('s', 'copy', yn_r[:, ct - 3:ct + 1, :],
                      PS[5][0:128, 0:512].rearrange("q (a t) -> q a t", a=4),
                      r=(PK[5],), w=(f'yn{cp}',))
            hb = hid_tg[cp]
            hkey = f'hid{cp}'
            DMA('L', hb[:], d_hid[3 + c * 128: 3 + (c + 1) * 128, :], w=(hkey,))
            DMA('L', xnl[cp][:], d_xnscr[c * 128:(c + 1) * 128, :],
                r=('xnscr',), w=(f'xnl{cp}',))
            DMA('L',
                q_fm[cp][:, :].rearrange("q (k c) -> q k c", k=8),
                d_qscr[:, c * 128:(c + 1) * 128].rearrange("(k q) c -> q k c", q=128),
                r=('qscr',), w=(f'qfm{cp}',))
            for half in range(2):
                for kt in range(8):
                    E('t', 'matmul', PS[6 + half][0:128, 0:512],
                      q_fm[cp][:, kt * 128:(kt + 1) * 128],
                      WffnT[:, kt * 1024 + half * 512: kt * 1024 + (half + 1) * 512],
                      start=(kt == 0), stop=(kt == 7),
                      r=(f'qfm{cp}', 'wffnT'), w=(PK[6 + half],))
            E('v', 'tensor_tensor', xsw[:], xs_all[:, c:c + 1], wscbc[:], ALU.mult,
              r=('xsall', 'wscbc'), w=('xsw',))
            for half in range(2):
                sl = slice(half * 512, (half + 1) * 512)
                E('v', 'scalar_tensor_tensor', db4[:, sl], PS[6 + half][0:128, 0:512],
                  xsw[:], biasbc[:, sl], ALU.mult, ALU.add,
                  r=(PK[6 + half], 'xsw', 'biasbc'), w=('db4',))
                E('s', 'activation', db2[:, sl], db4[:, sl], AF.Silu, r=('db4',), w=('db2',))
            for half in range(2):
                for ct in range(16):
                    E('t', 'matmul', PS[6 + half][0:128, 0:512],
                      yn_fm[cp][:, ct * 128:(ct + 1) * 128],
                      Wout_bf[:, ct * 1024 + half * 512: ct * 1024 + (half + 1) * 512],
                      start=(ct == 0), stop=(ct == 15),
                      r=(f'yn{cp}', 'wout'), w=(PK[6 + half],))
            ob = out_buf[cp]
            okey = f'ob{cp}'
            for half in range(2):
                sl = slice(half * 512, (half + 1) * 512)
                E('v', 'tensor_tensor', db3[:, sl], PS[6 + half][0:128, 0:512],
                  xnl[cp][:, sl], ALU.add, r=(PK[6 + half], f'xnl{cp}'), w=('db3',))
                E('v', 'tensor_tensor', db4[:, sl], db3[:, sl], db2[:, sl], ALU.mult,
                  r=('db3', 'db2'), w=('db4',))
                E('v', 'tensor_tensor', ob[:, sl], db4[:, sl], hb[:, sl], ALU.add,
                  r=('db4', hkey), w=(okey,))
            DMA('S', d_out[c * 128:(c + 1) * 128, :], ob[:], r=(okey,))
            if debug:
                DMA('S', d_dbg2[c * 128:(c + 1) * 128, :], yn_fm[cp][:], r=(f'yn{cp}',))
                DMA('S', d_dbg4[c * 128:(c + 1) * 128, :], db3[:], r=('db3',))
                DMA('S', d_dbg5[c * 128:(c + 1) * 128, :], db2[:], r=('db2',))

        for c in range(nch):
            scan_emit(c)
            if c >= 1:
                epi_emit(c - 1)
        epi_emit(nch - 1)

        for lab in QSEMS['S'] + QSEMS['L']:
            _wait('y', lab, cnt[lab])

        for g_ in reversed(groupB):
            g_.__exit__(None, None, None)

        with nc.Block() as blk:
            blk.sync(lambda e: [f(e) for f in progs['y']])
            blk.vector(lambda e: [f(e) for f in progs['v']])
            blk.scalar(lambda e: [f(e) for f in progs['s']])
            blk.tensor(lambda e: [f(e) for f in progs['t']])
            blk.gpsimd(lambda e: [f(e) for f in progs['g']])

    return nc


_NC_CACHE = {}


def _get_nc(tpc):
    if tpc not in _NC_CACHE:
        _NC_CACHE[tpc] = build_program(tpc)
    return _NC_CACHE[tpc]


def make_consts():
    I128f = np.eye(128, dtype=np.float32)
    I128b = np.eye(128, dtype=ml_dtypes.bfloat16)
    U128 = np.triu(np.ones((128, 128), dtype=np.float32))
    oneh = np.zeros((128, 64), dtype=np.float32)
    for j in range(8):
        oneh[:, j * 8 + j] = 1.0
    ones1 = np.ones((1, 128), dtype=np.float32)
    ccons = np.zeros((128, 6), dtype=np.float32)
    ccons[:, 0] = LN_EPS
    ccons[:, 1] = RMS_EPS
    ccons[:, 2] = 1.0
    ccons[127, 3] = 1.0
    ccons[:, 4] = 8388608.0
    ccons[:, 5] = -8388608.0
    icons_np = np.zeros((128, 4), dtype=np.int32)
    icons_np[:, 0] = 8388608
    icons_np[:, 1] = 1006632960
    icons_np[:, 2] = 1048576
    icons_np[:, 3] = -25165824
    return I128f, I128b, U128, oneh, ones1, ccons, icons_np


def make_in_maps(tpc, hidden, wni, win, cw, cb, wdt, bdt, alog, dh, wno, wout,
                 wffn, wsc, fbias):
    I128f, I128b, U128, oneh, ones1, ccons, icons_np = make_consts()
    in_maps = []
    for core in range(8):
        b, s = core // 4, core % 4
        lo = s * tpc - 3
        if lo < 0:
            hid_k = np.concatenate(
                [np.zeros((3, D), np.float32), hidden[b, 0:(s + 1) * tpc]], axis=0)
        else:
            hid_k = hidden[b, lo:(s + 1) * tpc]
        smask = np.array([[1.0 if j < s else 0.0 for j in range(4)]], np.float32)
        in_maps.append({
            "hid": np.ascontiguousarray(hid_k, np.float32),
            "w_norm_in": np.ascontiguousarray(np.asarray(wni, np.float32).reshape(8, 128).T),
            "w_in_proj": np.ascontiguousarray(win, np.float32),
            "conv_w": np.ascontiguousarray(
                np.asarray(cw, np.float32).reshape(16, 128, 4).transpose(1, 0, 2).reshape(128, 64)),
            "conv_b": np.ascontiguousarray(np.asarray(cb, np.float32).reshape(16, 128).T),
            "w_dt": np.ascontiguousarray(wdt, np.float32),
            "b_dt": np.ascontiguousarray(np.asarray(bdt, np.float32).reshape(16, 128).T),
            "A_log": np.ascontiguousarray(np.asarray(alog, np.float32).reshape(NH, 1)),
            "D_head": np.ascontiguousarray(np.asarray(dh, np.float32).reshape(1, NH)),
            "w_norm_out": np.ascontiguousarray(np.asarray(wno, np.float32).reshape(16, 128).T),
            "w_out_proj": np.ascontiguousarray(wout, np.float32),
            "ffn_w": np.ascontiguousarray(np.asarray(wffn).view(np.uint8)),
            "ffn_wscale": np.ascontiguousarray(np.asarray(wsc, np.float32).reshape(1, 1)),
            "ffn_bias": np.ascontiguousarray(np.asarray(fbias, np.float32).reshape(1, D)),
            "smask": smask,
            "I128f": I128f, "I128b": I128b, "U128": U128, "oneh": oneh,
            "ones1": ones1, "colconsts": ccons, "intconsts": icons_np,
        })
    return in_maps


def kernel(L0_attn_hidden, L0_attn_w_norm_in, L0_attn_w_in_proj, L0_attn_conv_w,
           L0_attn_conv_b, L0_attn_w_dt, L0_attn_b_dt, L0_attn_A_log, L0_attn_D,
           L0_attn_w_norm_out, L0_attn_w_out_proj, L0_ffn_weight, L0_ffn_weight_scale,
           L0_ffn_bias):
    hidden = np.asarray(L0_attn_hidden, np.float32)
    Bz, S, _ = hidden.shape
    tpc = S // 4
    nc = _get_nc(tpc)
    in_maps = make_in_maps(
        tpc, hidden, L0_attn_w_norm_in, L0_attn_w_in_proj, L0_attn_conv_w,
        L0_attn_conv_b, L0_attn_w_dt, L0_attn_b_dt, L0_attn_A_log, L0_attn_D,
        L0_attn_w_norm_out, L0_attn_w_out_proj, L0_ffn_weight, L0_ffn_weight_scale,
        L0_ffn_bias)
    res = run_bass_kernel_spmd(nc, in_maps, core_ids=list(range(8)))
    out = np.empty((Bz, S, D), np.float32)
    for core in range(8):
        b, s = core // 4, core % 4
        out[b, s * tpc:(s + 1) * tpc] = res.results[core]["out"]
    return out
